# version 20
# baseline (speedup 1.0000x reference)
import numpy as np
from contextlib import ExitStack

import concourse.bass as bass
import concourse.mybir as mybir
from concourse.bass_utils import run_bass_kernel_spmd

F32 = mybir.dt.float32
BF16 = mybir.dt.bfloat16
AF = mybir.ActivationFunctionType
ALU = mybir.AluOpType
AX = mybir.AxisListType

E = 2048
S = 2048
H = 16
HD = 128
NT = S // 128
NI = E // 128
NEXP = 8
SCALE = 1.0 / float(np.sqrt(HD))
CHUNK = 1024
NCH = S // CHUNK
EPS = 1e-5
NEG = -30000.0


class Sem:
    def __init__(self, ctx, nc, name):
        self.h = ctx.enter_context(nc.semaphore(name))
        self.n = 0

    def inc(self, instr, k=1):
        instr.then_inc(self.h, k)
        self.n += k
        return self.n


class RecInstr:

    def __init__(self):
        self.incs = []

    def then_inc(self, sem, k):
        self.incs.append((sem, k))
        return self


class Recorder:

    def __init__(self):
        self.calls = []

    def __getattr__(self, name):
        def call(*a, **kw):
            ri = RecInstr()
            self.calls.append((name, a, kw, ri))
            return ri
        return call


class Plan:
    ENGS = ("tensor", "vector", "scalar", "sync", "gpsimd")

    def __init__(self):
        self.q = {k: [] for k in self.ENGS}
        self.guards = {}
        self.auto = {}

    def set_auto(self, eng, sem):
        self.auto[eng] = sem

    def on(self, eng, fn):
        rec = Recorder()
        fn(rec)
        auto = self.auto.get(eng)
        base = None
        if auto is not None:
            base = auto.n
            for name, a, kw, ri in rec.calls:
                ri.incs = [(s, k) for (s, k) in ri.incs if s is not auto.h]
                if name not in ("wait_ge", "dma_start"):
                    auto.n += 1

        def replay(engine, calls=rec.calls, auto=auto, base=base):
            k = base
            for name, a, kw, ri in calls:
                if auto is not None and name != "wait_ge" and k > 0:
                    engine.wait_ge(auto.h, k)
                instr = getattr(engine, name)(*a, **kw)
                for sem, kk in ri.incs:
                    instr.then_inc(sem, kk)
                if auto is not None and name not in ("wait_ge", "dma_start"):
                    instr.then_inc(auto.h, 1)
                    k += 1
        self.q[eng].append(replay)

    def wait(self, eng, sem, val):
        if val > 0:
            self.q[eng].append(lambda e, s=sem.h, v=val: e.wait_ge(s, v))

    def guard_wait(self, eng, slot):
        g = self.guards.get(slot)
        if g is not None:
            self.wait(eng, g[0], g[1])

    def set_guard(self, slot, sem, val):
        self.guards[slot] = (sem, val)


def build(debug=False, n_tt=NT, n_jb=2 * H, n_vtb=NT, n_gtb=NT, n_h=H,
          n_p4=NT, n_ch=NCH):
    nc = bass.Bass("TRN2", target_bir_lowering=False)

    x_in = nc.dram_tensor("x", [S, E], F32, kind="ExternalInput").ap()
    w_qkv = nc.dram_tensor("w_qkv", [E, 3 * E], F32, kind="ExternalInput").ap()
    b_qkv = nc.dram_tensor("b_qkv", [3 * E], F32, kind="ExternalInput").ap()
    w_out = nc.dram_tensor("w_out", [E, E], F32, kind="ExternalInput").ap()
    b_out = nc.dram_tensor("b_out", [E], F32, kind="ExternalInput").ap()
    gate_w = nc.dram_tensor("gate_w", [E, NEXP], F32, kind="ExternalInput").ap()
    fc1_w = nc.dram_tensor("fc1_w", [NEXP, E, E], F32, kind="ExternalInput").ap()
    fc1_b = nc.dram_tensor("fc1_b", [NEXP, E], F32, kind="ExternalInput").ap()
    fc2_w = nc.dram_tensor("fc2_w", [NEXP, E, E], F32, kind="ExternalInput").ap()
    fc2_b = nc.dram_tensor("fc2_b", [NEXP, E], F32, kind="ExternalInput").ap()
    alpha = nc.dram_tensor("alpha", [NEXP], F32, kind="ExternalInput").ap()
    cmask = nc.dram_tensor("cmask", [128, 128], F32, kind="ExternalInput").ap()
    ident_in = nc.dram_tensor("ident", [128, 128], F32, kind="ExternalInput").ap()
    out = nc.dram_tensor("out", [S, E], F32, kind="ExternalOutput").ap()

    kind_dbg = "ExternalOutput" if debug else "Internal"
    xnT_d = nc.dram_tensor("xnT_d", [NI, 128, S], BF16, kind=kind_dbg).ap()
    qT_d = nc.dram_tensor("qT_d", [H, 128, S], BF16, kind=kind_dbg).ap()
    kT_d = nc.dram_tensor("kT_d", [H, 128, S], BF16, kind=kind_dbg).ap()
    v_d = nc.dram_tensor("v_d", [NT, 128, E], BF16, kind=kind_dbg).ap()
    ctxT_d = nc.dram_tensor("ctxT_d", [H, 128, S], BF16, kind=kind_dbg).ap()

    wqkv_v = w_qkv.rearrange("(it p) j -> p it j", p=128)
    wout_v = w_out.rearrange("(it p) j -> p it j", p=128)
    gate_v = gate_w.rearrange("(it p) e -> p it e", p=128)
    fc1_v = fc1_w.rearrange("e (dt p) i -> e p dt i", p=128)
    fc2_v = fc2_w.rearrange("e (it p) j -> e p it j", p=128)
    x_v = x_in.rearrange("(tt p) d -> tt p d", p=128)
    out_v = out.rearrange("(tt p) d -> tt p d", p=128)

    P = Plan()

    with ExitStack() as ctx:
        En = ctx.enter_context

        pe = Sem(ctx, nc, "pe")
        ve = Sem(ctx, nc, "ve")
        se = Sem(ctx, nc, "se")
        d_misc = Sem(ctx, nc, "d_misc")
        d_xA = Sem(ctx, nc, "d_xA")
        d_xB = Sem(ctx, nc, "d_xB")
        d_wA = Sem(ctx, nc, "d_wA")
        d_wB = Sem(ctx, nc, "d_wB")
        d_wv = Sem(ctx, nc, "d_wv")
        d_sp = Sem(ctx, nc, "d_sp")
        d_kA = Sem(ctx, nc, "d_kA")
        d_kB = Sem(ctx, nc, "d_kB")
        d_qA = Sem(ctx, nc, "d_qA")
        d_qB = Sem(ctx, nc, "d_qB")
        d_cA = Sem(ctx, nc, "d_cA")
        d_cB = Sem(ctx, nc, "d_cB")
        d_out = Sem(ctx, nc, "d_out")
        d_f1A = Sem(ctx, nc, "d_f1A")
        d_f1B = Sem(ctx, nc, "d_f1B")
        d_f1C = Sem(ctx, nc, "d_f1C")
        d_f1D = Sem(ctx, nc, "d_f1D")
        d_f2A = Sem(ctx, nc, "d_f2A")
        d_f2B = Sem(ctx, nc, "d_f2B")
        d_mc = Sem(ctx, nc, "d_mc")
        d_gb = Sem(ctx, nc, "d_gb")
        d_mk = Sem(ctx, nc, "d_mk")
        d_sp2 = Sem(ctx, nc, "d_sp2")
        pe2 = Sem(ctx, nc, "pe2")
        P.set_auto("vector", ve)
        P.set_auto("scalar", se)

        mask_sb = En(nc.sbuf_tensor("mask_sb", [128, 128], F32))
        maskb_sb = En(nc.sbuf_tensor("maskb_sb", [128, 128], BF16))
        ident_sb = En(nc.sbuf_tensor("ident_sb", [128, 128], BF16))
        bout_row = En(nc.sbuf_tensor("bout_row", [1, E], BF16))
        alpha_row = En(nc.sbuf_tensor("alpha_row", [1, NEXP], BF16))
        ones_row = En(nc.sbuf_tensor("ones_row", [1, S], BF16))
        gate_sb = En(nc.sbuf_tensor("gate_sb", [128, NI * NEXP], BF16))
        comb_sb = En(nc.sbuf_tensor("comb_sb", [128, NT * NEXP], F32))
        alphab_sb = En(nc.sbuf_tensor("alphab_sb", [128, NEXP], F32))
        ones_col = En(nc.sbuf_tensor("ones_col", [128, 1], BF16))
        kn2b_sb = En(nc.sbuf_tensor("kn2b_sb", [128, H], F32))
        psA = En(nc.psum_tensor("psA", [128, S], F32))
        psB = En(nc.psum_tensor("psB", [128, S], F32))

        def g0(gp):
            i = gp.dma_start(out=ident_sb[:, :], in_=ident_in)
            d_misc.inc(i, 16)
            i = gp.dma_start(out=maskb_sb[:, :], in_=cmask)
            d_misc.inc(i, 16)
            i = gp.dma_start(out=bout_row[:, :], in_=b_out[None, :])
            d_misc.inc(i, 16)
            i = gp.dma_start(out=alpha_row[:, :], in_=alpha[None, :])
            d_misc.inc(i, 16)
            for it in range(NI):
                i = gp.dma_start(out=gate_sb[:, it * 8:(it + 1) * 8],
                                 in_=gate_v[:, it, :])
                d_misc.inc(i, 16)
        P.on("gpsimd", g0)

        def s0(sy):
            i = sy.dma_start(out=mask_sb[:, :], in_=cmask)
            d_mk.inc(i, 16)
        P.on("sync", s0)
        mask_rdy = d_mk.n

        def v0(vv):
            vv.memset(ones_col[:, :], 1.0)
            i = vv.memset(ones_row[:, :], 1.0)
        P.on("vector", v0)
        ones_ve = ve.n

        a12 = ExitStack()
        xnT_sb = a12.enter_context(nc.sbuf_tensor("xnT_sb", [128, NI * S], BF16))
        wv_sb = a12.enter_context(nc.sbuf_tensor("wv_sb", [128, NI * E], BF16))
        bqkv_row = a12.enter_context(nc.sbuf_tensor("bqkv_row", [1, 3 * E], BF16))
        xt_sb = a12.enter_context(nc.sbuf_tensor("xt_sb", [128, 2 * E], F32))
        sq_sb = a12.enter_context(nc.sbuf_tensor("sq_sb", [128, E], F32))
        xnt_sb = a12.enter_context(nc.sbuf_tensor("xnt_sb", [128, 2 * E], BF16))
        wqk_sb = a12.enter_context(nc.sbuf_tensor("wqk_sb", [128, 2 * E], BF16))
        ev_sb = a12.enter_context(nc.sbuf_tensor("ev_sb", [128, 2 * E], BF16))
        st12 = a12.enter_context(nc.sbuf_tensor("st12", [128, 16], F32))
        eg_sb = a12.enter_context(nc.sbuf_tensor("eg_sb", [128, 6 * NEXP], F32))
        k2_sb = a12.enter_context(nc.sbuf_tensor("k2_sb", [128, S], BF16))
        knm_sb = a12.enter_context(nc.sbuf_tensor("knm_sb", [1, 1], BF16))

        def g_bq(gp):
            i = gp.dma_start(out=bqkv_row[:, :], in_=b_qkv[None, :])
            d_misc.inc(i, 16)
        P.on("gpsimd", g_bq)
        misc_n = d_misc.n

        def v_z(vv):
            vv.memset(st12[:, :], 0.0)
        P.on("vector", v_z)
        stz_ve = ve.n
        if debug:
            def v_z2(vv):
                vv.memset(xnT_sb[:, :], 0.0)
                vv.memset(ev_sb[:, :], 0.0)
                vv.memset(xnt_sb[:, :], 0.0)
            P.on("vector", v_z2)

        ps_tr = psA[:, 0:1024].bitcast(BF16)

        xnT_view = xnT_sb[:, :].rearrange("p (dt s) -> p dt s", dt=NI)
        for tt in range(n_tt):
            sl = tt % 2
            dx = d_xA if sl == 0 else d_xB
            xt = xt_sb[:, sl * E:(sl + 1) * E]
            xnt = xnt_sb[:, sl * E:(sl + 1) * E]
            c0 = sl * 8
            st = lambda k, c0=c0: st12[:, c0 + k:c0 + k + 1]

            P.guard_wait("sync", f"xt{sl}")

            def s_ld(sy, tt=tt, dx=dx, xt=xt):
                i = sy.dma_start(out=xt, in_=x_v[tt, :, :])
                dx.inc(i, 16)
            P.on("sync", s_ld)
            x_rdy = dx.n

            P.wait("scalar", dx, x_rdy)
            if tt < 2:
                P.wait("scalar", ve, stz_ve)

            def sc_sq(sc, xt=xt, st=st):
                i = sc.activation(sq_sb[:, :], xt, AF.Square, accum_out=st(2))
            P.on("scalar", sc_sq)
            ssq_se = se.n

            P.wait("vector", dx, x_rdy)
            P.wait("vector", se, ssq_se)

            def v_stats(vv, xt=xt, st=st):
                vv.reduce_sum(st(0), xt, axis=AX.X)
                vv.tensor_scalar(st(1), st(0), 1.0 / E, None, ALU.mult)
                vv.tensor_scalar(st(3), st(2), 1.0 / E, None, ALU.mult)
                vv.tensor_mul(st(4), st(1), st(1))
                vv.tensor_sub(st(4), st(3), st(4))
                vv.tensor_scalar(st(4), st(4), EPS, None, ALU.add)
                i = vv.reciprocal(st(5), st(4))
            P.on("vector", v_stats)
            rv_ve = ve.n

            P.wait("scalar", ve, rv_ve)

            def sc_rstd(sc, st=st):
                i = sc.activation(st(6), st(5), AF.Sqrt)
            P.on("scalar", sc_rstd)
            rstd_se = se.n

            P.wait("vector", se, rstd_se)

            def v_nms(vv, st=st):
                i = vv.scalar_tensor_tensor(st(7), st(1), -1.0, st(6),
                                            ALU.mult, ALU.mult)
            P.on("vector", v_nms)
            nms_ve = ve.n

            P.wait("vector", ve, nms_ve)

            def v_xn(vv, xt=xt, xnt=xnt, st=st):
                i = vv.tensor_scalar(xnt, xt, st(6), st(7), ALU.mult, ALU.add)
            P.on("vector", v_xn)
            xn_ve = ve.n
            P.set_guard(f"xt{sl}", ve, xn_ve)

            P.wait("tensor", ve, xn_ve)
            if tt == 0:
                P.wait("tensor", d_misc, misc_n)
            for grp in range(4):
                ph = grp % 2
                P.guard_wait("tensor", f"ps_tr{ph}")

                def t_tr(t, grp=grp, ph=ph, xnt=xnt):
                    for k in range(4):
                        i = t.transpose(ps_tr[:, ph * 1024 + k * 128:
                                              ph * 1024 + (k + 1) * 128],
                                        xnt[:, (grp * 4 + k) * 128:
                                            (grp * 4 + k + 1) * 128],
                                        ident_sb[:, :])
                    pe.inc(i)
                P.on("tensor", t_tr)
                tr_pe = pe.n

                P.wait("vector", pe, tr_pe)

                def v_ev(vv, grp=grp, ph=ph, tt=tt):
                    src = ps_tr[:, ph * 1024:ph * 1024 + 512]
                    src = src.rearrange("p (k c) -> p k c", k=4)
                    dst = xnT_view[:, grp * 4:grp * 4 + 4,
                                   tt * 128:(tt + 1) * 128]
                    i = vv.tensor_copy(dst, src)
                P.on("vector", v_ev)
                P.set_guard(f"ps_tr{ph}", ve, ve.n)
        xnT_ve = ve.n
        if debug:
            dbg_xnt = nc.dram_tensor("dbg_xnt", [128, E], BF16,
                                     kind="ExternalOutput").ap()
            dbg_st = nc.dram_tensor("dbg_st", [128, 16], F32,
                                    kind="ExternalOutput").ap()
            P.wait("sync", ve, xnT_ve)

            def s_dbg1(sy):
                i = sy.dma_start(out=dbg_xnt, in_=xnt_sb[:, 0:E])
                d_sp.inc(i, 16)
                i = sy.dma_start(out=dbg_st, in_=st12[:, :])
                d_sp.inc(i, 16)
            P.on("sync", s_dbg1)
        ps_qk = [psA, psB]

        for jb in range(n_jb):
            sl = jb % 2
            dw = d_wA if sl == 0 else d_wB
            wsl = wqk_sb[:, sl * E:(sl + 1) * E]
            esl = ev_sb[:, sl * E:(sl + 1) * E]
            ps = ps_qk[sl]
            j0 = jb * 128

            P.guard_wait("gpsimd", f"wqk{sl}")

            def g_w(gp, wsl=wsl, dw=dw, j0=j0):
                i = gp.dma_start(out=wsl.rearrange("p (it j) -> p it j", it=NI),
                                 in_=wqkv_v[:, :, j0:j0 + 128])
                dw.inc(i, 16)
            P.on("gpsimd", g_w)
            w_rdy = dw.n

            P.wait("tensor", dw, w_rdy)
            if jb < 2:
                P.wait("tensor", ve, xnT_ve)
                P.wait("tensor", ve, ones_ve)
            P.guard_wait("tensor", f"ps_qk{sl}")

            def t_qk(t, wsl=wsl, ps=ps, j0=j0):
                for it in range(NI):
                    lhsT = wsl[:, it * 128:(it + 1) * 128]
                    for nb in range(4):
                        t.matmul(ps[:, nb * 512:(nb + 1) * 512], lhsT,
                                 xnT_sb[:, it * S + nb * 512:
                                        it * S + (nb + 1) * 512],
                                 start=(it == 0), stop=False)
                for nb in range(4):
                    i = t.matmul(ps[:, nb * 512:(nb + 1) * 512],
                                 bqkv_row[0:1, j0:j0 + 128],
                                 ones_row[0:1, nb * 512:(nb + 1) * 512],
                                 start=False, stop=True)
                pe.inc(i)
            P.on("tensor", t_qk)
            qk_pe = pe.n
            P.set_guard(f"wqk{sl}", pe, qk_pe)

            P.wait("scalar", pe, qk_pe)
            P.wait("scalar", d_sp, d_sp.n)

            def sc_ev(sc, ps=ps, esl=esl, jb=jb):
                i = sc.activation(esl, ps[:, :], AF.Copy)
                dst = qT_d[jb, :, :] if jb < H else kT_d[jb - H, :, :]
                i = sc.dma_start(out=dst, in_=esl)
                d_sp.inc(i, 16)
            P.on("scalar", sc_ev)
            evac_se = se.n
            if jb >= H:
                hh = jb - H
                P.wait("vector", se, evac_se)

                def v_k2(vv, esl=esl):
                    i = vv.tensor_mul(k2_sb[:, :], esl, esl)
                P.on("vector", v_k2)
                k2_ve = ve.n
                P.wait("tensor", ve, k2_ve)

                def t_kn(t, ps=ps):
                    for nb in range(4):
                        i = t.matmul(ps[0:1, nb * 512:(nb + 1) * 512],
                                     ones_col[:, :],
                                     k2_sb[:, nb * 512:(nb + 1) * 512],
                                     start=True, stop=True)
                    pe.inc(i)
                P.on("tensor", t_kn)
                kn_pe = pe.n
                P.wait("vector", pe, kn_pe)

                def v_knm(vv, ps=ps):
                    i = vv.reduce_max(knm_sb[0:1, 0:1], ps[0:1, 0:S], axis=AX.X)
                P.on("vector", v_knm)
                knm_ve = ve.n
                P.wait("tensor", ve, knm_ve)

                def t_knb(t, ps=ps):
                    i = t.matmul(ps[:, 0:1], ones_row[0:1, 0:128],
                                 knm_sb[0:1, 0:1], start=True, stop=True)
                    pe.inc(i)
                P.on("tensor", t_knb)
                knb_pe = pe.n
                P.wait("vector", pe, knb_pe)

                def v_knb(vv, ps=ps, hh=hh):
                    i = vv.tensor_copy(kn2b_sb[:, hh:hh + 1], ps[:, 0:1])
                P.on("vector", v_knb)
                P.set_guard(f"ps_qk{sl}", ve, ve.n)
            else:
                P.set_guard(f"ps_qk{sl}", se, evac_se)

        def g_wv(gp):
            for it in range(NI):
                i = gp.dma_start(out=wv_sb[:, it * E:(it + 1) * E],
                                 in_=wqkv_v[:, it, 2 * E:3 * E])
                d_wv.inc(i, 16)
        P.on("gpsimd", g_wv)
        wv_rdy = d_wv.n

        for tb in range(n_vtb):
            sl = tb % 2
            ps = ps_qk[sl]
            esl = ev_sb[:, sl * E:(sl + 1) * E]

            if tb < 2:
                P.wait("tensor", d_wv, wv_rdy)
            P.guard_wait("tensor", f"ps_qk{sl}")

            def t_v(t, tb=tb, ps=ps):
                for it in range(NI):
                    lhsT = xnT_sb[:, it * S + tb * 128:it * S + (tb + 1) * 128]
                    for nb in range(4):
                        t.matmul(ps[:, nb * 512:(nb + 1) * 512], lhsT,
                                 wv_sb[:, it * E + nb * 512:
                                       it * E + (nb + 1) * 512],
                                 start=(it == 0), stop=False)
                for nb in range(4):
                    i = t.matmul(ps[:, nb * 512:(nb + 1) * 512],
                                 ones_row[0:1, 0:128],
                                 bqkv_row[0:1, 2 * E + nb * 512:
                                          2 * E + (nb + 1) * 512],
                                 start=False, stop=True)
                pe.inc(i)
            P.on("tensor", t_v)
            v_pe = pe.n

            P.wait("scalar", pe, v_pe)
            P.wait("scalar", d_sp, d_sp.n)

            def sc_vev(sc, ps=ps, esl=esl, tb=tb):
                i = sc.activation(esl, ps[:, :], AF.Copy)
                i = sc.dma_start(out=v_d[tb, :, :], in_=esl)
                d_sp.inc(i, 16)
            P.on("scalar", sc_vev)
            P.set_guard(f"ps_qk{sl}", se, se.n)

        P.guard_wait("tensor", "ps_qk0")
        P.wait("tensor", d_misc, misc_n)
        P.wait("tensor", ve, xnT_ve)

        def t_ab(t):
            i = t.matmul(ps_qk[0][:, 0:8], ones_row[0:1, 0:128],
                         alpha_row[0:1, :], start=True, stop=True)
            pe.inc(i)
        P.on("tensor", t_ab)
        ab_pe = pe.n
        P.wait("vector", pe, ab_pe)
        P.wait("vector", d_misc, misc_n)

        def v_ab(vv):
            i = vv.tensor_copy(alphab_sb[:, :], ps_qk[0][:, 0:8])
        P.on("vector", v_ab)
        P.set_guard("ps_qk0", ve, ve.n)

        gcol = lambda k: eg_sb[:, k * 8:(k + 1) * 8]
        for tb in range(n_gtb):
            sl = tb % 2
            ps = ps_qk[sl]
            P.guard_wait("tensor", f"ps_qk{sl}")

            def t_g(t, tb=tb, ps=ps):
                for it in range(NI):
                    i = t.matmul(ps[:, 0:8],
                                 xnT_sb[:, it * S + tb * 128:
                                        it * S + (tb + 1) * 128],
                                 gate_sb[:, it * 8:(it + 1) * 8],
                                 start=(it == 0), stop=(it == NI - 1))
                pe.inc(i)
            P.on("tensor", t_g)
            g_pe = pe.n

            P.wait("vector", pe, g_pe)
            P.guard_wait("vector", "gate_cols")

            def v_g1(vv, ps=ps):
                vv.tensor_copy(gcol(0), ps[:, 0:8])
                vv.reduce_max(st12[:, 0:1], gcol(0), axis=AX.X)
                i = vv.reduce_max(st12[:, 1:2], gcol(0), axis=AX.X, negate=True)
            P.on("vector", v_g1)
            g1_ve = ve.n
            P.set_guard(f"ps_qk{sl}", ve, g1_ve)

            P.wait("scalar", ve, g1_ve)

            def sc_g(sc):
                i = sc.activation(gcol(1), gcol(0), AF.Exp, bias=st12[:, 1:2])
            P.on("scalar", sc_g)
            g_se = se.n

            P.wait("vector", se, g_se)

            def v_g2(vv):
                vv.tensor_scalar(gcol(2), gcol(0), st12[:, 0:1], None,
                                 ALU.is_equal)
                vv.scalar_tensor_tensor(gcol(2), gcol(2), -1e30, gcol(0),
                                        ALU.mult, ALU.add)
                i = vv.reduce_max(st12[:, 2:3], gcol(2), axis=AX.X)
            P.on("vector", v_g2)
            P.wait("vector", ve, ve.n)

            def v_g3(vv):
                vv.tensor_scalar(gcol(3), gcol(0), st12[:, 2:3], None, ALU.is_ge)
                vv.tensor_mul(gcol(1), gcol(1), gcol(3))
                i = vv.reduce_sum(st12[:, 3:4], gcol(1), axis=AX.X)
            P.on("vector", v_g3)
            P.wait("vector", ve, ve.n)

            def v_g4(vv):
                i = vv.reciprocal(st12[:, 4:5], st12[:, 3:4])
            P.on("vector", v_g4)
            P.wait("vector", ve, ve.n)

            def v_g5(vv, tb=tb):
                vv.tensor_scalar(gcol(1), gcol(1), st12[:, 4:5], None, ALU.mult)
                i = vv.tensor_mul(comb_sb[:, tb * 8:(tb + 1) * 8],
                                  gcol(1), alphab_sb[:, :])
            P.on("vector", v_g5)
            P.set_guard("gate_cols", ve, ve.n)

        P.wait("scalar", pe, pe.n)

        def sc_spxn(sc):
            i = sc.dma_start(out=xnT_d.rearrange("n p s -> p n s"),
                             in_=xnT_sb[:, :].rearrange("p (n s) -> p n s",
                                                        n=NI))
            d_sp.inc(i, 16)
        P.on("scalar", sc_spxn)
        sp_all = d_sp.n

        p2_pe, p2_ve, p2_se = pe.n, ve.n, se.n
        a12.close()

        a3 = ExitStack()
        kt_sb = a3.enter_context(nc.sbuf_tensor("kt_sb", [128, 2 * S], BF16))
        vh_sb = a3.enter_context(nc.sbuf_tensor("vh_sb", [128, 2 * S], BF16))
        qh_sb = a3.enter_context(nc.sbuf_tensor("qh_sb", [128, 2 * S], BF16))
        p_sb = a3.enter_context(nc.sbuf_tensor("p_sb", [128, 8 * S], BF16))
        pT_sb = a3.enter_context(nc.sbuf_tensor("pT_sb", [128, 16 * 512], BF16))
        cte_sb = a3.enter_context(nc.sbuf_tensor("cte_sb", [128, 2 * 512], BF16))
        st3 = a3.enter_context(nc.sbuf_tensor("st3", [128, 16], F32))
        q2_sb = a3.enter_context(nc.sbuf_tensor("q2_sb", [128, S], BF16))
        bnd_sb = a3.enter_context(nc.sbuf_tensor("bnd_sb", [128, 2 * NT], F32))
        ps_S = psA
        ps_tr3 = psB[:, 0:1024].bitcast(BF16)
        ps_ctx = psB[:, 1024:2048]

        for eng in ("tensor", "vector", "scalar", "sync"):
            P.wait(eng, pe, p2_pe)
            P.wait(eng, ve, p2_ve)
            P.wait(eng, se, p2_se)
        P.wait("sync", d_sp, sp_all)
        P.wait("vector", d_mk, mask_rdy)

        pT_view = pT_sb[:, :].rearrange("p (k c) -> p k c", k=16)

        groups = [(h, g) for h in range(n_h) for g in range(4)]
        pendingB = []

        def flushB(n):
            for _ in range(min(n, len(pendingB))):
                pendingB.pop(0)()

        def plan_kv_load(h):
            ksl = h % 2
            dk = d_kA if ksl == 0 else d_kB
            kth = kt_sb[:, ksl * S:(ksl + 1) * S]
            vh = vh_sb[:, ksl * S:(ksl + 1) * S]
            P.guard_wait("sync", f"kv{ksl}")

            def s_kv(sy, h=h, kth=kth, vh=vh, dk=dk):
                i = sy.dma_start(out=kth, in_=kT_d[h, :, :])
                dk.inc(i, 16)
                i = sy.dma_start(out=vh.rearrange("p (k c) -> p k c", k=NT),
                                 in_=v_d[:, :, h * 128:(h + 1) * 128]
                                 .rearrange("k p c -> p k c"))
                dk.inc(i, 16)
            P.on("sync", s_kv)
            return dk.n

        def plan_q_load(hq):
            qsl = hq % 2
            dq = d_qA if qsl == 0 else d_qB
            qh = qh_sb[:, qsl * S:(qsl + 1) * S]
            P.guard_wait("sync", f"q{qsl}")

            def s_q(sy, hq=hq, qh=qh, dq=dq):
                i = sy.dma_start(out=qh, in_=qT_d[hq, :, :])
                dq.inc(i, 16)
            P.on("sync", s_q)
            return dq.n

        def plan_bnd(hh):
            hpar = hh % 2
            dqh = d_qA if hpar == 0 else d_qB
            qhf = qh_sb[:, hpar * S:(hpar + 1) * S]
            P.wait("vector", dqh, q_rdys[hh])
            P.guard_wait("vector", "q2buf")

            def v_q2(vv, qhf=qhf):
                i = vv.tensor_mul(q2_sb[:, :], qhf, qhf)
            P.on("vector", v_q2)
            q2_ve = ve.n
            P.wait("tensor", ve, q2_ve)
            P.guard_wait("tensor", "ps_S0")

            def t_qn(t):
                for tb in range(NT):
                    i = t.matmul(ps_S[:, tb:tb + 1],
                                 q2_sb[:, tb * 128:(tb + 1) * 128],
                                 ones_col[:, :], start=True, stop=True)
                pe.inc(i)
            P.on("tensor", t_qn)
            qn_pe = pe.n
            P.set_guard("q2buf", pe, qn_pe)
            P.wait("vector", pe, qn_pe)
            P.guard_wait("vector", f"bndbuf{hpar}")

            def v_bnd(vv, hh=hh, hpar=hpar):
                i = vv.tensor_scalar(bnd_sb[:, hpar * NT:(hpar + 1) * NT],
                                     ps_S[:, 0:NT],
                                     kn2b_sb[:, hh:hh + 1], -0.5 * SCALE,
                                     ALU.add, ALU.mult)
            P.on("vector", v_bnd)
            bnd_ve = ve.n
            P.set_guard("ps_S0", ve, bnd_ve)
            P.wait("scalar", ve, bnd_ve)

        kv_rdys = {0: plan_kv_load(0)}
        q_rdys = {0: plan_q_load(0)}
        bank_cur = 0

        for gi, (h, g) in enumerate(groups):
            par = gi % 2
            ksl = h % 2
            dk = d_kA if ksl == 0 else d_kB
            vh = vh_sb[:, ksl * S:(ksl + 1) * S]
            kth = kt_sb[:, ksl * S:(ksl + 1) * S]

            if g == 1 and h + 1 < n_h:
                kv_rdys[h + 1] = plan_kv_load(h + 1)
                q_rdys[h + 1] = plan_q_load(h + 1)
            kv_rdy = kv_rdys[h]
            q_rdy = q_rdys[h]
            qsl = h % 2
            dq = d_qA if qsl == 0 else d_qB
            qg = qh_sb[:, qsl * S + g * 512:qsl * S + (g + 1) * 512]

            if gi == 0:
                plan_bnd(0)
            if g == 2 and h + 1 < n_h:
                plan_bnd(h + 1)

            P.guard_wait("scalar", f"pbuf{par}")

            for qbi in range(4):
                qb = 4 * g + qbi
                K = (qb + 1) * 128
                nbs = (K + 511) // 512
                pq = p_sb[:, (par * 4 + qbi) * S:(par * 4 + qbi) * S + S]

                P.wait("tensor", dq, q_rdy)
                P.wait("tensor", dk, kv_rdy)

                ab = 8 + 4 * (qbi % 2)
                P.guard_wait("scalar", f"lcols{qbi % 2}")
                for nb in range(nbs):
                    n0 = nb * 512
                    n1 = min(K, n0 + 512)
                    bk = (bank_cur + nb) % 4
                    b0 = bk * 512
                    diag = (n1 == K)
                    P.guard_wait("tensor", f"ps_S{bk}")

                    def t_s(t, qbi=qbi, qb=qb, n0=n0, n1=n1, b0=b0, qg=qg,
                            kth=kth, diag=diag):
                        i = t.matmul(ps_S[:, b0:b0 + (n1 - n0)],
                                     qg[:, qbi * 128:(qbi + 1) * 128],
                                     kth[:, n0:n1], start=True, stop=not diag)
                        if diag:
                            d0 = b0 + qb * 128 - n0
                            i = t.matmul(ps_S[:, d0:d0 + 128], ident_sb[:, :],
                                         maskb_sb[:, :], start=False,
                                         stop=True)
                        pe.inc(i)
                    P.on("tensor", t_s)
                    s_pe = pe.n
                    P.wait("scalar", pe, s_pe)

                    def sc_exp(sc, pq=pq, n0=n0, n1=n1, b0=b0, qb=qb, nb=nb,
                               ab=ab):
                        i = sc.activation(pq[:, n0:n1],
                                          ps_S[:, b0:b0 + (n1 - n0)],
                                          AF.Exp,
                                          bias=bnd_sb[:, (h % 2) * NT + qb:
                                                      (h % 2) * NT + qb + 1],
                                          scale=SCALE,
                                          accum_out=st3[:, ab + nb:ab + nb + 1])
                    P.on("scalar", sc_exp)
                    P.set_guard(f"ps_S{bk}", se, se.n)
                bank_cur = (bank_cur + nbs) % 4
                if qbi == 3:
                    P.set_guard(f"q{qsl}", pe, pe.n)
                flushB(1)

                exp_se = se.n
                if qbi == 3 and g == 3:
                    P.set_guard(f"bndbuf{h % 2}", se, exp_se)
                P.wait("vector", se, exp_se)

                def v_li(vv, nbs=nbs, ab=ab, qbi=qbi):
                    a = st3[:, ab:ab + 1]
                    for nb in range(1, nbs):
                        vv.tensor_add(a, a, st3[:, ab + nb:ab + nb + 1])
                    i = vv.reciprocal(st3[:, 2 + qbi % 2:3 + qbi % 2], a)
                P.on("vector", v_li)
                P.set_guard(f"lcols{qbi % 2}", ve, ve.n)
                P.wait("vector", ve, ve.n)

                def v_ps2(vv, pq=pq, K=K, qbi=qbi):
                    i = vv.tensor_scalar(pq[:, 0:K], pq[:, 0:K],
                                         st3[:, 2 + qbi % 2:3 + qbi % 2],
                                         None, ALU.mult)
                P.on("vector", v_ps2)

            pscale_ve = ve.n

            def mk_tr_piece(kts, h=h, g=g, par=par, pscale_ve=pscale_ve,
                            last=False):
                def piece():
                    P.wait("tensor", ve, pscale_ve)
                    for kt in kts:
                        q0 = max(0, kt - 4 * g)
                        cnt = 4 - q0
                        ph = kt % 2
                        P.guard_wait("tensor", f"ps_tr3{ph}")

                        def t_ptr(t, kt=kt, q0=q0, cnt=cnt, ph=ph, par=par):
                            for idx in range(cnt):
                                qbi = q0 + idx
                                sl = (par * 4 + qbi) * S
                                i = t.transpose(
                                    ps_tr3[:, ph * 1024 + idx * 128:
                                           ph * 1024 + (idx + 1) * 128],
                                    p_sb[:, sl + kt * 128:sl + (kt + 1) * 128],
                                    ident_sb[:, :])
                            pe.inc(i)
                        P.on("tensor", t_ptr)
                        ptr_pe = pe.n

                        P.wait("vector", pe, ptr_pe)

                        def v_pev(vv, kt=kt, q0=q0, cnt=cnt, ph=ph):
                            i = vv.tensor_copy(
                                pT_view[:, kt, q0 * 128:(q0 + cnt) * 128],
                                ps_tr3[:, ph * 1024:ph * 1024 + cnt * 128])
                        P.on("vector", v_pev)
                        P.set_guard(f"ps_tr3{ph}", ve, ve.n)
                    if last:
                        P.set_guard(f"pbuf{par}", pe, pe.n)
                return piece

            def mk_ctx_piece(h=h, g=g, gi=gi, vh=vh, ksl=ksl):
                def piece():
                    nkt = 4 * g + 4
                    csl = gi % 2
                    P.wait("tensor", ve, ve.n)
                    P.guard_wait("tensor", f"ps_ctx{csl}")

                    def t_ctx(t, g=g, nkt=nkt, vh=vh, csl=csl):
                        for kt in range(nkt):
                            q0 = max(0, kt - 4 * g)
                            i = t.matmul(
                                ps_ctx[:, csl * 512 + q0 * 128:csl * 512 + 512],
                                vh[:, kt * 128:(kt + 1) * 128],
                                pT_view[:, kt, q0 * 128:512],
                                start=(kt == 0), stop=(kt == nkt - 1))
                        pe.inc(i)
                    P.on("tensor", t_ctx)
                    if g == 3:
                        P.set_guard(f"kv{ksl}", pe, pe.n)
                return piece

            def mk_ev_piece(h=h, g=g, gi=gi):
                def piece():
                    csl = gi % 2
                    ctx_pe = pe.n
                    P.wait("vector", pe, ctx_pe)
                    P.wait("vector", d_sp2, d_sp2.n)

                    def v_cev(vv, csl=csl):
                        i = vv.tensor_copy(cte_sb[:, csl * 512:(csl + 1) * 512],
                                           ps_ctx[:, csl * 512:(csl + 1) * 512])
                    P.on("vector", v_cev)
                    cev_ve = ve.n
                    P.set_guard(f"ps_ctx{csl}", ve, cev_ve)

                    P.wait("gpsimd", ve, cev_ve)

                    def s_cts(gp, h=h, g=g, csl=csl):
                        i = gp.dma_start(out=ctxT_d[h, :,
                                                    g * 512:(g + 1) * 512],
                                         in_=cte_sb[:, csl * 512:
                                                    (csl + 1) * 512])
                        d_sp2.inc(i, 16)
                    P.on("gpsimd", s_cts)
                return piece

            nkt = 4 * g + 4
            half = (nkt + 1) // 2
            flushB(len(pendingB))
            pendingB = [
                mk_tr_piece(list(range(half))),
                mk_tr_piece(list(range(half, nkt)), last=True),
                mk_ctx_piece(),
                mk_ev_piece(),
            ]
            if gi == len(groups) - 1:
                flushB(4)
        p3_sp = d_sp.n
        p3_sp2 = d_sp2.n

        p3_pe, p3_ve, p3_se = pe.n, ve.n, se.n
        a3.close()

        a4 = ExitStack()
        wo_sb = a4.enter_context(nc.sbuf_tensor("wo_sb", [128, NI * E], BF16))
        ct_sb = a4.enter_context(nc.sbuf_tensor("ct_sb", [128, 2 * E], BF16))
        x4_sb = a4.enter_context(nc.sbuf_tensor("x4_sb", [128, 2 * E], F32))
        o4_sb = a4.enter_context(nc.sbuf_tensor("o4_sb", [128, 2 * E], F32))
        ps_o = [psA, psB]

        for eng in ("tensor", "vector", "scalar", "sync", "gpsimd"):
            P.wait(eng, pe, p3_pe)
            P.wait(eng, ve, p3_ve)
            P.wait(eng, se, p3_se)
        P.wait("gpsimd", d_sp, p3_sp)
        P.wait("gpsimd", d_sp2, p3_sp2)

        def g_wo(gp):
            for it in range(NI):
                i = gp.dma_start(out=wo_sb[:, it * E:(it + 1) * E],
                                 in_=wout_v[:, it, :])
                d_wv.inc(i, 16)
        P.on("gpsimd", g_wo)
        wo_rdy = d_wv.n

        P.wait("sync", d_sp, p3_sp)
        P.wait("sync", d_sp2, p3_sp2)
        for tb in range(n_p4):
            sl = tb % 2
            dc = d_cA if sl == 0 else d_cB
            dx = d_xA if sl == 0 else d_xB
            ctv = ct_sb[:, sl * E:(sl + 1) * E].rearrange(
                "p (k c) -> p k c", k=NI)
            x4 = x4_sb[:, sl * E:(sl + 1) * E]
            o4 = o4_sb[:, sl * E:(sl + 1) * E]
            ps = ps_o[sl]

            P.guard_wait("sync", f"ct{sl}")

            def s_ct(sy, tb=tb, ctv=ctv, dc=dc):
                i = sy.dma_start(out=ctv,
                                 in_=ctxT_d[:, :, tb * 128:(tb + 1) * 128]
                                 .rearrange("k p c -> p k c"))
                dc.inc(i, 16)
            P.on("sync", s_ct)
            ct_rdy = dc.n

            P.guard_wait("sync", f"x4{sl}")

            def s_x4(sy, tb=tb, x4=x4, dx=dx):
                i = sy.dma_start(out=x4, in_=x_v[tb, :, :])
                dx.inc(i, 16)
            P.on("sync", s_x4)
            x4_rdy = dx.n

            P.wait("tensor", dc, ct_rdy)
            if tb < 2:
                P.wait("tensor", d_wv, wo_rdy)
            P.guard_wait("tensor", f"ps_o{sl}")

            def t_o(t, ctv=ctv, ps=ps):
                for it in range(NI):
                    lhsT = ctv[:, it, :]
                    for nb in range(4):
                        t.matmul(ps[:, nb * 512:(nb + 1) * 512], lhsT,
                                 wo_sb[:, it * E + nb * 512:
                                       it * E + (nb + 1) * 512],
                                 start=(it == 0), stop=False)
                for nb in range(4):
                    i = t.matmul(ps[:, nb * 512:(nb + 1) * 512],
                                 ones_row[0:1, 0:128],
                                 bout_row[0:1, nb * 512:(nb + 1) * 512],
                                 start=False, stop=True)
                pe.inc(i)
            P.on("tensor", t_o)
            o_pe = pe.n
            P.set_guard(f"ct{sl}", pe, o_pe)

            P.wait("vector", pe, o_pe)
            P.wait("vector", dx, x4_rdy)
            P.wait("vector", d_out, d_out.n)

            def v_o(vv, ps=ps, x4=x4, o4=o4):
                i = vv.tensor_add(o4, ps[:, :], x4)
            P.on("vector", v_o)
            o_ve = ve.n
            P.set_guard(f"ps_o{sl}", ve, o_ve)
            P.set_guard(f"x4{sl}", ve, o_ve)

            P.wait("sync", ve, o_ve)

            def s_o(sy, tb=tb, o4=o4):
                i = sy.dma_start(out=out_v[tb, :, :], in_=o4)
                d_out.inc(i, 16)
            P.on("sync", s_o)
        p4_out = d_out.n

        p4_pe, p4_ve, p4_se = pe.n, ve.n, se.n
        a4.close()

        a5 = ExitStack()
        xnc_sb = a5.enter_context(nc.sbuf_tensor("xnc_sb", [128, NI * CHUNK], BF16))
        acc_sb = a5.enter_context(nc.sbuf_tensor("acc_sb", [128, 8 * E], F32))
        hT_sb = a5.enter_context(nc.sbuf_tensor("hT_sb", [128, NI * CHUNK], BF16))
        f2_sb = a5.enter_context(nc.sbuf_tensor("f2_sb", [128, 8 * E], BF16))
        f1_sb = a5.enter_context(nc.sbuf_tensor("f1_sb", [128, 4 * E], BF16))
        b1_row = a5.enter_context(nc.sbuf_tensor("b1_row", [1, E], BF16))
        b2_row = a5.enter_context(nc.sbuf_tensor("b2_row", [1, E], BF16))
        ps_h = psA
        ps_f = psB

        for eng in ("tensor", "vector", "scalar", "sync", "gpsimd"):
            P.wait(eng, pe, p4_pe)
            P.wait(eng, ve, p4_ve)
            P.wait(eng, se, p4_se)

        out_ch = out.rearrange("(tb p) j -> tb p j", p=128)

        for c in range(n_ch):
            P.guard_wait("sync", "xnc")
            P.wait("sync", d_sp, d_sp.n)

            def s_xnc(sy, c=c):
                i = sy.dma_start(
                    out=xnc_sb[:, :].rearrange("p (n s) -> p n s", n=NI),
                    in_=xnT_d[:, :, c * CHUNK:(c + 1) * CHUNK]
                    .rearrange("n p s -> p n s"))
                d_mc.inc(i, 16)
            P.on("sync", s_xnc)
            xnc_rdy = d_mc.n

            P.wait("sync", d_out, d_out.n)

            def s_acc(sy, c=c):
                i = sy.dma_start(
                    out=acc_sb[:, :].rearrange("p (tb j) -> p tb j", tb=8),
                    in_=out_ch[c * 8:(c + 1) * 8, :, :]
                    .rearrange("t p j -> p t j"))
                d_mc.inc(i, 16)
            P.on("sync", s_acc)
            acc_rdy = d_mc.n
            xnc_rdy = d_mc.n
            P.wait("vector", d_mc, acc_rdy)

            for e in range(NEXP):
                P.guard_wait("gpsimd", "b12")

                def g_b(gp, e=e):
                    i = gp.dma_start(out=b1_row[:, :], in_=fc1_b[e, :][None, :])
                    d_gb.inc(i, 16)
                    i = gp.dma_start(out=b2_row[:, :], in_=fc2_b[e, :][None, :])
                    d_gb.inc(i, 16)
                P.on("gpsimd", g_b)
                b_rdy = d_gb.n

                f1_rdys = {}

                def plan_f1_fill(ib, e=e):
                    sl = ib % 4
                    df = (d_f1A, d_f1B, d_f1C, d_f1D)[sl]
                    f1 = f1_sb[:, sl * E:(sl + 1) * E]
                    P.guard_wait("gpsimd", f"f1{sl}")

                    def g_f1(gp, e=e, ib=ib, f1=f1, df=df):
                        i = gp.dma_start(
                            out=f1.rearrange("p (dt j) -> p dt j", dt=NI),
                            in_=fc1_v[e, :, :, ib * 128:(ib + 1) * 128])
                        df.inc(i, 16)
                    P.on("gpsimd", g_f1)
                    f1_rdys[ib] = df.n

                def plan_f2_tile(half, it8, dfh, e=e):
                    if it8 == 0:
                        P.guard_wait("gpsimd", "f2buf")

                    def g_f2(gp, e=e, half=half, it8=it8, dfh=dfh):
                        i = gp.dma_start(
                            out=f2_sb[:, it8 * E:(it8 + 1) * E],
                            in_=fc2_v[e, :, half * 8 + it8, :])
                        dfh.inc(i, 16)
                    P.on("gpsimd", g_f2)
                    return dfh.n

                def plan_fc1(ib, e=e):
                    sl = ib % 4
                    df = (d_f1A, d_f1B, d_f1C, d_f1D)[sl]
                    f1 = f1_sb[:, sl * E:(sl + 1) * E]

                    P.wait("tensor", df, f1_rdys[ib])
                    if e == 0 and ib < 2:
                        P.wait("tensor", d_mc, xnc_rdy)
                    if ib == 0:
                        P.wait("tensor", d_gb, b_rdy)
                    psl = ib % 2
                    P.guard_wait("tensor", f"ps_h{psl}")

                    def t_f1(t, ib=ib, f1=f1, psl=psl):
                        psh = ps_h[:, psl * 1024:(psl + 1) * 1024]
                        for dt in range(NI):
                            lhsT = f1[:, dt * 128:(dt + 1) * 128]
                            for nb in range(2):
                                t.matmul(psh[:, nb * 512:(nb + 1) * 512], lhsT,
                                         xnc_sb[:, dt * CHUNK + nb * 512:
                                                dt * CHUNK + (nb + 1) * 512],
                                         start=(dt == 0), stop=False)
                        for nb in range(2):
                            i = t.matmul(psh[:, nb * 512:(nb + 1) * 512],
                                         b1_row[0:1, ib * 128:(ib + 1) * 128],
                                         ones_row[0:1, nb * 512:(nb + 1) * 512],
                                         start=False, stop=True)
                        pe.inc(i)
                    P.on("tensor", t_f1)
                    f1_pe = pe.n
                    P.set_guard(f"f1{sl}", pe, f1_pe)

                    P.wait("scalar", pe, f1_pe)
                    P.guard_wait("scalar", "hT_lo" if ib < 8 else "hT_hi")

                    def sc_gel(sc, ib=ib, psl=psl):
                        i = sc.activation(hT_sb[:, ib * CHUNK:(ib + 1) * CHUNK],
                                          ps_h[:, psl * 1024:(psl + 1) * 1024],
                                          AF.Gelu)
                    P.on("scalar", sc_gel)
                    P.set_guard(f"ps_h{psl}", se, se.n)

                def plan_fc2(half, f2_rdy, gel_se, e=e, c=c):
                    dfh = d_f2A if half == 0 else d_f2B
                    P.wait("tensor", dfh, f2_rdy)
                    P.wait("tensor", se, gel_se)
                    for tb in range(8):
                        for jh in range(2):
                            P.guard_wait("tensor", f"ps_f{jh}")

                            def t_f2(t, tb=tb, half=half, jh=jh):
                                psf = ps_f[:, jh * 1024:(jh + 1) * 1024]
                                last = None
                                for it8 in range(8):
                                    it = half * 8 + it8
                                    lhsT = hT_sb[:, it * CHUNK + tb * 128:
                                                 it * CHUNK + (tb + 1) * 128]
                                    for nb in range(2):
                                        j0 = jh * 1024 + nb * 512
                                        stop = (half == 1 and it8 == 7)
                                        last = t.matmul(
                                            psf[:, nb * 512:(nb + 1) * 512],
                                            lhsT,
                                            f2_sb[:, it8 * E + j0:
                                                  it8 * E + j0 + 512],
                                            start=(it8 == 0), stop=stop)
                                if half == 0:
                                    for nb in range(2):
                                        j0 = jh * 1024 + nb * 512
                                        last = t.matmul(
                                            psf[:, nb * 512:(nb + 1) * 512],
                                            ones_row[0:1, 0:128],
                                            b2_row[0:1, j0:j0 + 512],
                                            start=False, stop=True)
                                pe.inc(last)
                            P.on("tensor", t_f2)
                            f2_pe = pe.n

                            P.wait("vector", pe, f2_pe)

                            def v_acc(vv, c=c, e=e, tb=tb, jh=jh):
                                cidx = (c * 8 + tb) * 8 + e
                                csl = comb_sb[:, cidx:cidx + 1]
                                at = acc_sb[:, tb * E + jh * 1024:
                                            tb * E + (jh + 1) * 1024]
                                i = vv.scalar_tensor_tensor(
                                    at, ps_f[:, jh * 1024:(jh + 1) * 1024],
                                    csl, at, ALU.mult, ALU.add)
                            P.on("vector", v_acc)
                            P.set_guard(f"ps_f{jh}", ve, ve.n)
                    P.set_guard("f2buf", pe, pe.n)
                    P.set_guard("hT_lo" if half == 0 else "hT_hi", pe, pe.n)

                for ib in range(4):
                    plan_f1_fill(ib)
                f2_rdy0 = plan_f2_tile(0, 0, d_f2A)
                for k in range(8):
                    plan_fc1(k)
                    if k < 4:
                        plan_f1_fill(k + 4)
                    if k < 7:
                        f2_rdy0 = plan_f2_tile(0, k + 1, d_f2A)
                gel_lo = se.n
                plan_fc2(0, f2_rdy0, gel_lo)

                for ib in range(8, 12):
                    plan_f1_fill(ib)
                f2_rdy1 = plan_f2_tile(1, 0, d_f2B)
                for k in range(8, NI):
                    plan_fc1(k)
                    if k < 12:
                        plan_f1_fill(k + 4)
                    if k < 15:
                        f2_rdy1 = plan_f2_tile(1, k - 7, d_f2B)
                gel_hi = se.n
                P.set_guard("b12", pe, pe.n)
                plan_fc2(1, f2_rdy1, gel_hi)

            acc_ve = ve.n
            P.wait("sync", ve, acc_ve)

            def s_fin(sy, c=c):
                i = sy.dma_start(out=out_ch[c * 8:(c + 1) * 8, :, :]
                                 .rearrange("t p j -> p t j"),
                                 in_=acc_sb[:, :].rearrange(
                                     "p (tb j) -> p tb j", tb=8))
                d_out.inc(i, 16)
            P.on("sync", s_fin)
            P.set_guard("xnc", pe, pe.n)

        a5.close()

        P.wait("sync", d_out, d_out.n)
        P.wait("sync", d_sp, d_sp.n)

        with nc.Block() as block:
            @block.tensor
            def _(t):
                for fn in P.q["tensor"]:
                    fn(t)

            @block.vector
            def _(v):
                for fn in P.q["vector"]:
                    fn(v)

            @block.scalar
            def _(s):
                for fn in P.q["scalar"]:
                    fn(s)

            @block.sync
            def _(s):
                for fn in P.q["sync"]:
                    fn(s)

            @block.gpsimd
            def _(g):
                for fn in P.q["gpsimd"]:
                    fn(g)

    return nc


_MASK = None
_IDENT = None


def _consts():
    global _MASK, _IDENT
    if _MASK is None:
        m = np.zeros((128, 128), dtype=np.float32)
        iu = np.triu_indices(128, k=1)
        m[iu] = NEG
        _MASK = m
        _IDENT = np.eye(128, dtype=np.float32)
    return _MASK, _IDENT


_NC_CACHE = None


def kernel(L0_attn_inX, L0_attn_W_qkv, L0_attn_b_qkv, L0_attn_W_out,
           L0_attn_b_out, L0_ffn_gate_w, L0_ffn_fc1_w, L0_ffn_fc1_b,
           L0_ffn_fc2_w, L0_ffn_fc2_b, L0_ffn_alpha):
    global _NC_CACHE
    x = np.ascontiguousarray(np.asarray(L0_attn_inX, dtype=np.float32))
    mask, ident = _consts()
    shared = {
        "w_qkv": np.ascontiguousarray(np.asarray(L0_attn_W_qkv, np.float32)),
        "b_qkv": np.ascontiguousarray(np.asarray(L0_attn_b_qkv, np.float32)),
        "w_out": np.ascontiguousarray(np.asarray(L0_attn_W_out, np.float32)),
        "b_out": np.ascontiguousarray(np.asarray(L0_attn_b_out, np.float32)),
        "gate_w": np.ascontiguousarray(np.asarray(L0_ffn_gate_w, np.float32)),
        "fc1_w": np.ascontiguousarray(np.asarray(L0_ffn_fc1_w, np.float32)),
        "fc1_b": np.ascontiguousarray(np.asarray(L0_ffn_fc1_b, np.float32)),
        "fc2_w": np.ascontiguousarray(np.asarray(L0_ffn_fc2_w, np.float32)),
        "fc2_b": np.ascontiguousarray(np.asarray(L0_ffn_fc2_b, np.float32)),
        "alpha": np.ascontiguousarray(np.asarray(L0_ffn_alpha, np.float32)),
        "cmask": mask, "ident": ident,
    }
    if _NC_CACHE is None:
        _NC_CACHE = build()
    nc = _NC_CACHE
    in_maps = [dict(shared, x=np.ascontiguousarray(x[b])) for b in range(8)]
    res = run_bass_kernel_spmd(nc, in_maps, core_ids=list(range(8)))
    return np.stack([r["out"] for r in res.results], axis=0)


# version 21 (final)
# speedup vs baseline: 1.2108x; 1.2108x over previous
import numpy as np
from contextlib import ExitStack

import concourse.bass as bass
import concourse.mybir as mybir
from concourse.bass_utils import run_bass_kernel_spmd

F32 = mybir.dt.float32
BF16 = mybir.dt.bfloat16
AF = mybir.ActivationFunctionType
ALU = mybir.AluOpType
AX = mybir.AxisListType

E = 2048
S = 2048
H = 16
HD = 128
NT = S // 128
NI = E // 128
NEXP = 8
SCALE = 1.0 / float(np.sqrt(HD))
CHUNK = 1024
NCH = S // CHUNK
EPS = 1e-5
NEG = -30000.0


class Sem:
    def __init__(self, ctx, nc, name):
        self.h = ctx.enter_context(nc.semaphore(name))
        self.n = 0

    def inc(self, instr, k=1):
        instr.then_inc(self.h, k)
        self.n += k
        return self.n


class RecInstr:

    def __init__(self):
        self.incs = []

    def then_inc(self, sem, k):
        self.incs.append((sem, k))
        return self


class Recorder:

    def __init__(self):
        self.calls = []

    def __getattr__(self, name):
        def call(*a, **kw):
            ri = RecInstr()
            self.calls.append((name, a, kw, ri))
            return ri
        return call


class Plan:
    ENGS = ("tensor", "vector", "scalar", "sync", "gpsimd")

    def __init__(self):
        self.q = {k: [] for k in self.ENGS}
        self.guards = {}
        self.auto = {}

    def set_auto(self, eng, sem):
        self.auto[eng] = sem

    def on(self, eng, fn):
        rec = Recorder()
        fn(rec)
        auto = self.auto.get(eng)
        base = None
        if auto is not None:
            base = auto.n
            for name, a, kw, ri in rec.calls:
                ri.incs = [(s, k) for (s, k) in ri.incs if s is not auto.h]
                if name not in ("wait_ge", "dma_start"):
                    auto.n += 1

        def replay(engine, calls=rec.calls, auto=auto, base=base):
            k = base
            for name, a, kw, ri in calls:
                if auto is not None and name != "wait_ge" and k > 0:
                    engine.wait_ge(auto.h, k)
                instr = getattr(engine, name)(*a, **kw)
                for sem, kk in ri.incs:
                    instr.then_inc(sem, kk)
                if auto is not None and name not in ("wait_ge", "dma_start"):
                    instr.then_inc(auto.h, 1)
                    k += 1
        self.q[eng].append(replay)

    def wait(self, eng, sem, val):
        if val > 0:
            self.q[eng].append(lambda e, s=sem.h, v=val: e.wait_ge(s, v))

    def guard_wait(self, eng, slot):
        g = self.guards.get(slot)
        if g is not None:
            self.wait(eng, g[0], g[1])

    def set_guard(self, slot, sem, val):
        self.guards[slot] = (sem, val)


def build(debug=False, n_tt=NT, n_jb=2 * H, n_vtb=NT, n_gtb=NT, n_h=H,
          n_p4=NT, n_ch=NCH):
    nc = bass.Bass("TRN2", target_bir_lowering=False)

    x_in = nc.dram_tensor("x", [S, E], F32, kind="ExternalInput").ap()
    w_qkv = nc.dram_tensor("w_qkv", [E, 3 * E], F32, kind="ExternalInput").ap()
    b_qkv = nc.dram_tensor("b_qkv", [3 * E], F32, kind="ExternalInput").ap()
    w_out = nc.dram_tensor("w_out", [E, E], F32, kind="ExternalInput").ap()
    b_out = nc.dram_tensor("b_out", [E], F32, kind="ExternalInput").ap()
    gate_w = nc.dram_tensor("gate_w", [E, NEXP], F32, kind="ExternalInput").ap()
    fc1_w = nc.dram_tensor("fc1_w", [NEXP, E, E], F32, kind="ExternalInput").ap()
    fc1_b = nc.dram_tensor("fc1_b", [NEXP, E], F32, kind="ExternalInput").ap()
    fc2_w = nc.dram_tensor("fc2_w", [NEXP, E, E], F32, kind="ExternalInput").ap()
    fc2_b = nc.dram_tensor("fc2_b", [NEXP, E], F32, kind="ExternalInput").ap()
    alpha = nc.dram_tensor("alpha", [NEXP], F32, kind="ExternalInput").ap()
    cmask = nc.dram_tensor("cmask", [128, 128], F32, kind="ExternalInput").ap()
    ident_in = nc.dram_tensor("ident", [128, 128], F32, kind="ExternalInput").ap()
    out = nc.dram_tensor("out", [S, E], F32, kind="ExternalOutput").ap()

    kind_dbg = "ExternalOutput" if debug else "Internal"
    xnT_d = nc.dram_tensor("xnT_d", [NI, 128, S], BF16, kind=kind_dbg).ap()
    qT_d = nc.dram_tensor("qT_d", [H, 128, S], BF16, kind=kind_dbg).ap()
    kT_d = nc.dram_tensor("kT_d", [H, 128, S], BF16, kind=kind_dbg).ap()
    v_d = nc.dram_tensor("v_d", [NT, 128, E], BF16, kind=kind_dbg).ap()
    ctxT_d = nc.dram_tensor("ctxT_d", [H, 128, S], BF16, kind=kind_dbg).ap()

    wqkv_v = w_qkv.rearrange("(it p) j -> p it j", p=128)
    wout_v = w_out.rearrange("(it p) j -> p it j", p=128)
    gate_v = gate_w.rearrange("(it p) e -> p it e", p=128)
    fc1_v = fc1_w.rearrange("e (dt p) i -> e p dt i", p=128)
    fc2_v = fc2_w.rearrange("e (it p) j -> e p it j", p=128)
    x_v = x_in.rearrange("(tt p) d -> tt p d", p=128)
    out_v = out.rearrange("(tt p) d -> tt p d", p=128)

    P = Plan()

    with ExitStack() as ctx:
        En = ctx.enter_context

        pe = Sem(ctx, nc, "pe")
        ve = Sem(ctx, nc, "ve")
        se = Sem(ctx, nc, "se")
        d_misc = Sem(ctx, nc, "d_misc")
        d_xA = Sem(ctx, nc, "d_xA")
        d_xB = Sem(ctx, nc, "d_xB")
        d_wA = Sem(ctx, nc, "d_wA")
        d_wB = Sem(ctx, nc, "d_wB")
        d_wv = Sem(ctx, nc, "d_wv")
        d_sp = Sem(ctx, nc, "d_sp")
        d_kA = Sem(ctx, nc, "d_kA")
        d_kB = Sem(ctx, nc, "d_kB")
        d_qA = Sem(ctx, nc, "d_qA")
        d_qB = Sem(ctx, nc, "d_qB")
        d_cA = Sem(ctx, nc, "d_cA")
        d_cB = Sem(ctx, nc, "d_cB")
        d_out = Sem(ctx, nc, "d_out")
        d_f1A = Sem(ctx, nc, "d_f1A")
        d_f1B = Sem(ctx, nc, "d_f1B")
        d_f1C = Sem(ctx, nc, "d_f1C")
        d_f1D = Sem(ctx, nc, "d_f1D")
        d_f2A = Sem(ctx, nc, "d_f2A")
        d_f2B = Sem(ctx, nc, "d_f2B")
        d_mc = Sem(ctx, nc, "d_mc")
        d_gb = Sem(ctx, nc, "d_gb")
        d_mk = Sem(ctx, nc, "d_mk")
        d_sp2 = Sem(ctx, nc, "d_sp2")
        pe2 = Sem(ctx, nc, "pe2")
        P.set_auto("vector", ve)
        P.set_auto("scalar", se)

        mask_sb = En(nc.sbuf_tensor("mask_sb", [128, 128], F32))
        maskb_sb = En(nc.sbuf_tensor("maskb_sb", [128, 128], BF16))
        ident_sb = En(nc.sbuf_tensor("ident_sb", [128, 128], BF16))
        bout_row = En(nc.sbuf_tensor("bout_row", [1, E], BF16))
        alpha_row = En(nc.sbuf_tensor("alpha_row", [1, NEXP], BF16))
        ones_row = En(nc.sbuf_tensor("ones_row", [1, S], BF16))
        gate_sb = En(nc.sbuf_tensor("gate_sb", [128, NI * NEXP], BF16))
        comb_sb = En(nc.sbuf_tensor("comb_sb", [128, NT * NEXP], F32))
        alphab_sb = En(nc.sbuf_tensor("alphab_sb", [128, NEXP], F32))
        ones_col = En(nc.sbuf_tensor("ones_col", [128, 1], BF16))
        kn2b_sb = En(nc.sbuf_tensor("kn2b_sb", [128, H], F32))
        psA = En(nc.psum_tensor("psA", [128, S], F32))
        psB = En(nc.psum_tensor("psB", [128, S], F32))

        def g0(gp):
            i = gp.dma_start(out=ident_sb[:, :], in_=ident_in)
            d_misc.inc(i, 16)
            i = gp.dma_start(out=maskb_sb[:, :], in_=cmask)
            d_misc.inc(i, 16)
            i = gp.dma_start(out=bout_row[:, :], in_=b_out[None, :])
            d_misc.inc(i, 16)
            i = gp.dma_start(out=alpha_row[:, :], in_=alpha[None, :])
            d_misc.inc(i, 16)
            for it in range(NI):
                i = gp.dma_start(out=gate_sb[:, it * 8:(it + 1) * 8],
                                 in_=gate_v[:, it, :])
                d_misc.inc(i, 16)
        P.on("gpsimd", g0)

        def s0(sy):
            i = sy.dma_start(out=mask_sb[:, :], in_=cmask)
            d_mk.inc(i, 16)
        P.on("sync", s0)
        mask_rdy = d_mk.n

        def v0(vv):
            vv.memset(ones_col[:, :], 1.0)
            i = vv.memset(ones_row[:, :], 1.0)
        P.on("vector", v0)
        ones_ve = ve.n

        a12 = ExitStack()
        xnT_sb = a12.enter_context(nc.sbuf_tensor("xnT_sb", [128, NI * S], BF16))
        wv_sb = a12.enter_context(nc.sbuf_tensor("wv_sb", [128, NI * E], BF16))
        bqkv_row = a12.enter_context(nc.sbuf_tensor("bqkv_row", [1, 3 * E], BF16))
        xt_sb = a12.enter_context(nc.sbuf_tensor("xt_sb", [128, 2 * E], F32))
        sq_sb = a12.enter_context(nc.sbuf_tensor("sq_sb", [128, E], F32))
        xnt_sb = a12.enter_context(nc.sbuf_tensor("xnt_sb", [128, 2 * E], BF16))
        wqk_sb = a12.enter_context(nc.sbuf_tensor("wqk_sb", [128, 2 * E], BF16))
        ev_sb = a12.enter_context(nc.sbuf_tensor("ev_sb", [128, 2 * E], BF16))
        st12 = a12.enter_context(nc.sbuf_tensor("st12", [128, 16], F32))
        eg_sb = a12.enter_context(nc.sbuf_tensor("eg_sb", [128, 6 * NEXP], F32))
        k2_sb = a12.enter_context(nc.sbuf_tensor("k2_sb", [128, S], BF16))
        knm_sb = a12.enter_context(nc.sbuf_tensor("knm_sb", [1, 1], BF16))

        def g_bq(gp):
            i = gp.dma_start(out=bqkv_row[:, :], in_=b_qkv[None, :])
            d_misc.inc(i, 16)
        P.on("gpsimd", g_bq)
        misc_n = d_misc.n

        def v_z(vv):
            vv.memset(st12[:, :], 0.0)
        P.on("vector", v_z)
        stz_ve = ve.n
        if debug:
            def v_z2(vv):
                vv.memset(xnT_sb[:, :], 0.0)
                vv.memset(ev_sb[:, :], 0.0)
                vv.memset(xnt_sb[:, :], 0.0)
            P.on("vector", v_z2)

        ps_tr = psA[:, 0:1024].bitcast(BF16)

        xnT_view = xnT_sb[:, :].rearrange("p (dt s) -> p dt s", dt=NI)
        for tt in range(n_tt):
            sl = tt % 2
            dx = d_xA if sl == 0 else d_xB
            xt = xt_sb[:, sl * E:(sl + 1) * E]
            xnt = xnt_sb[:, sl * E:(sl + 1) * E]
            c0 = sl * 8
            st = lambda k, c0=c0: st12[:, c0 + k:c0 + k + 1]

            P.guard_wait("sync", f"xt{sl}")

            def s_ld(sy, tt=tt, dx=dx, xt=xt):
                i = sy.dma_start(out=xt, in_=x_v[tt, :, :])
                dx.inc(i, 16)
            P.on("sync", s_ld)
            x_rdy = dx.n

            P.wait("scalar", dx, x_rdy)
            if tt < 2:
                P.wait("scalar", ve, stz_ve)

            def sc_sq(sc, xt=xt, st=st):
                i = sc.activation(sq_sb[:, :], xt, AF.Square, accum_out=st(2))
            P.on("scalar", sc_sq)
            ssq_se = se.n

            P.wait("vector", dx, x_rdy)
            P.wait("vector", se, ssq_se)

            def v_stats(vv, xt=xt, st=st):
                vv.reduce_sum(st(0), xt, axis=AX.X)
                vv.tensor_scalar(st(1), st(0), 1.0 / E, None, ALU.mult)
                vv.tensor_scalar(st(3), st(2), 1.0 / E, None, ALU.mult)
                vv.tensor_mul(st(4), st(1), st(1))
                vv.tensor_sub(st(4), st(3), st(4))
                vv.tensor_scalar(st(4), st(4), EPS, None, ALU.add)
                i = vv.reciprocal(st(5), st(4))
            P.on("vector", v_stats)
            rv_ve = ve.n

            P.wait("scalar", ve, rv_ve)

            def sc_rstd(sc, st=st):
                i = sc.activation(st(6), st(5), AF.Sqrt)
            P.on("scalar", sc_rstd)
            rstd_se = se.n

            P.wait("vector", se, rstd_se)

            def v_nms(vv, st=st):
                i = vv.scalar_tensor_tensor(st(7), st(1), -1.0, st(6),
                                            ALU.mult, ALU.mult)
            P.on("vector", v_nms)
            nms_ve = ve.n

            P.wait("vector", ve, nms_ve)

            def v_xn(vv, xt=xt, xnt=xnt, st=st):
                i = vv.tensor_scalar(xnt, xt, st(6), st(7), ALU.mult, ALU.add)
            P.on("vector", v_xn)
            xn_ve = ve.n
            P.set_guard(f"xt{sl}", ve, xn_ve)

            P.wait("tensor", ve, xn_ve)
            if tt == 0:
                P.wait("tensor", d_misc, misc_n)
            for grp in range(4):
                ph = grp % 2
                P.guard_wait("tensor", f"ps_tr{ph}")

                def t_tr(t, grp=grp, ph=ph, xnt=xnt):
                    for k in range(4):
                        i = t.transpose(ps_tr[:, ph * 1024 + k * 128:
                                              ph * 1024 + (k + 1) * 128],
                                        xnt[:, (grp * 4 + k) * 128:
                                            (grp * 4 + k + 1) * 128],
                                        ident_sb[:, :])
                    pe.inc(i)
                P.on("tensor", t_tr)
                tr_pe = pe.n

                P.wait("vector", pe, tr_pe)

                def v_ev(vv, grp=grp, ph=ph, tt=tt):
                    src = ps_tr[:, ph * 1024:ph * 1024 + 512]
                    src = src.rearrange("p (k c) -> p k c", k=4)
                    dst = xnT_view[:, grp * 4:grp * 4 + 4,
                                   tt * 128:(tt + 1) * 128]
                    i = vv.tensor_copy(dst, src)
                P.on("vector", v_ev)
                P.set_guard(f"ps_tr{ph}", ve, ve.n)
        xnT_ve = ve.n
        if debug:
            dbg_xnt = nc.dram_tensor("dbg_xnt", [128, E], BF16,
                                     kind="ExternalOutput").ap()
            dbg_st = nc.dram_tensor("dbg_st", [128, 16], F32,
                                    kind="ExternalOutput").ap()
            P.wait("sync", ve, xnT_ve)

            def s_dbg1(sy):
                i = sy.dma_start(out=dbg_xnt, in_=xnt_sb[:, 0:E])
                d_sp.inc(i, 16)
                i = sy.dma_start(out=dbg_st, in_=st12[:, :])
                d_sp.inc(i, 16)
            P.on("sync", s_dbg1)
        ps_qk = [psA, psB]

        for jb in range(n_jb):
            sl = jb % 2
            dw = d_wA if sl == 0 else d_wB
            wsl = wqk_sb[:, sl * E:(sl + 1) * E]
            esl = ev_sb[:, sl * E:(sl + 1) * E]
            ps = ps_qk[sl]
            j0 = jb * 128

            P.guard_wait("gpsimd", f"wqk{sl}")

            def g_w(gp, wsl=wsl, dw=dw, j0=j0):
                i = gp.dma_start(out=wsl.rearrange("p (it j) -> p it j", it=NI),
                                 in_=wqkv_v[:, :, j0:j0 + 128])
                dw.inc(i, 16)
            P.on("gpsimd", g_w)
            w_rdy = dw.n

            P.wait("tensor", dw, w_rdy)
            if jb < 2:
                P.wait("tensor", ve, xnT_ve)
                P.wait("tensor", ve, ones_ve)
            P.guard_wait("tensor", f"ps_qk{sl}")

            def t_qk(t, wsl=wsl, ps=ps, j0=j0):
                for it in range(NI):
                    lhsT = wsl[:, it * 128:(it + 1) * 128]
                    for nb in range(4):
                        t.matmul(ps[:, nb * 512:(nb + 1) * 512], lhsT,
                                 xnT_sb[:, it * S + nb * 512:
                                        it * S + (nb + 1) * 512],
                                 start=(it == 0), stop=False)
                for nb in range(4):
                    i = t.matmul(ps[:, nb * 512:(nb + 1) * 512],
                                 bqkv_row[0:1, j0:j0 + 128],
                                 ones_row[0:1, nb * 512:(nb + 1) * 512],
                                 start=False, stop=True)
                pe.inc(i)
            P.on("tensor", t_qk)
            qk_pe = pe.n
            P.set_guard(f"wqk{sl}", pe, qk_pe)

            P.wait("scalar", pe, qk_pe)
            P.wait("scalar", d_sp, d_sp.n)

            def sc_ev(sc, ps=ps, esl=esl, jb=jb):
                i = sc.activation(esl, ps[:, :], AF.Copy)
                dst = qT_d[jb, :, :] if jb < H else kT_d[jb - H, :, :]
                i = sc.dma_start(out=dst, in_=esl)
                d_sp.inc(i, 16)
            P.on("scalar", sc_ev)
            evac_se = se.n
            if jb >= H:
                hh = jb - H
                P.wait("vector", se, evac_se)

                def v_k2(vv, esl=esl):
                    i = vv.tensor_mul(k2_sb[:, :], esl, esl)
                P.on("vector", v_k2)
                k2_ve = ve.n
                P.wait("tensor", ve, k2_ve)

                def t_kn(t, ps=ps):
                    for nb in range(4):
                        i = t.matmul(ps[0:1, nb * 512:(nb + 1) * 512],
                                     ones_col[:, :],
                                     k2_sb[:, nb * 512:(nb + 1) * 512],
                                     start=True, stop=True)
                    pe.inc(i)
                P.on("tensor", t_kn)
                kn_pe = pe.n
                P.wait("vector", pe, kn_pe)

                def v_knm(vv, ps=ps):
                    i = vv.reduce_max(knm_sb[0:1, 0:1], ps[0:1, 0:S], axis=AX.X)
                P.on("vector", v_knm)
                knm_ve = ve.n
                P.wait("tensor", ve, knm_ve)

                def t_knb(t, ps=ps):
                    i = t.matmul(ps[:, 0:1], ones_row[0:1, 0:128],
                                 knm_sb[0:1, 0:1], start=True, stop=True)
                    pe.inc(i)
                P.on("tensor", t_knb)
                knb_pe = pe.n
                P.wait("vector", pe, knb_pe)

                def v_knb(vv, ps=ps, hh=hh):
                    i = vv.tensor_copy(kn2b_sb[:, hh:hh + 1], ps[:, 0:1])
                P.on("vector", v_knb)
                P.set_guard(f"ps_qk{sl}", ve, ve.n)
            else:
                P.set_guard(f"ps_qk{sl}", se, evac_se)

        def g_wv(gp):
            for it in range(NI):
                i = gp.dma_start(out=wv_sb[:, it * E:(it + 1) * E],
                                 in_=wqkv_v[:, it, 2 * E:3 * E])
                d_wv.inc(i, 16)
        P.on("gpsimd", g_wv)
        wv_rdy = d_wv.n

        for tb in range(n_vtb):
            sl = tb % 2
            ps = ps_qk[sl]
            esl = ev_sb[:, sl * E:(sl + 1) * E]

            if tb < 2:
                P.wait("tensor", d_wv, wv_rdy)
            P.guard_wait("tensor", f"ps_qk{sl}")

            def t_v(t, tb=tb, ps=ps):
                for it in range(NI):
                    lhsT = xnT_sb[:, it * S + tb * 128:it * S + (tb + 1) * 128]
                    for nb in range(4):
                        t.matmul(ps[:, nb * 512:(nb + 1) * 512], lhsT,
                                 wv_sb[:, it * E + nb * 512:
                                       it * E + (nb + 1) * 512],
                                 start=(it == 0), stop=False)
                for nb in range(4):
                    i = t.matmul(ps[:, nb * 512:(nb + 1) * 512],
                                 ones_row[0:1, 0:128],
                                 bqkv_row[0:1, 2 * E + nb * 512:
                                          2 * E + (nb + 1) * 512],
                                 start=False, stop=True)
                pe.inc(i)
            P.on("tensor", t_v)
            v_pe = pe.n

            P.wait("scalar", pe, v_pe)
            P.wait("scalar", d_sp, d_sp.n)

            def sc_vev(sc, ps=ps, esl=esl, tb=tb):
                i = sc.activation(esl, ps[:, :], AF.Copy)
                i = sc.dma_start(out=v_d[tb, :, :], in_=esl)
                d_sp.inc(i, 16)
            P.on("scalar", sc_vev)
            P.set_guard(f"ps_qk{sl}", se, se.n)

        P.guard_wait("tensor", "ps_qk0")
        P.wait("tensor", d_misc, misc_n)
        P.wait("tensor", ve, xnT_ve)

        def t_ab(t):
            i = t.matmul(ps_qk[0][:, 0:8], ones_row[0:1, 0:128],
                         alpha_row[0:1, :], start=True, stop=True)
            pe.inc(i)
        P.on("tensor", t_ab)
        ab_pe = pe.n
        P.wait("vector", pe, ab_pe)
        P.wait("vector", d_misc, misc_n)

        def v_ab(vv):
            i = vv.tensor_copy(alphab_sb[:, :], ps_qk[0][:, 0:8])
        P.on("vector", v_ab)
        P.set_guard("ps_qk0", ve, ve.n)

        gcol = lambda k: eg_sb[:, k * 8:(k + 1) * 8]
        for tb in range(n_gtb):
            sl = tb % 2
            ps = ps_qk[sl]
            P.guard_wait("tensor", f"ps_qk{sl}")

            def t_g(t, tb=tb, ps=ps):
                for it in range(NI):
                    i = t.matmul(ps[:, 0:8],
                                 xnT_sb[:, it * S + tb * 128:
                                        it * S + (tb + 1) * 128],
                                 gate_sb[:, it * 8:(it + 1) * 8],
                                 start=(it == 0), stop=(it == NI - 1))
                pe.inc(i)
            P.on("tensor", t_g)
            g_pe = pe.n

            P.wait("vector", pe, g_pe)
            P.guard_wait("vector", "gate_cols")

            def v_g1(vv, ps=ps):
                vv.tensor_copy(gcol(0), ps[:, 0:8])
                vv.reduce_max(st12[:, 0:1], gcol(0), axis=AX.X)
                i = vv.reduce_max(st12[:, 1:2], gcol(0), axis=AX.X, negate=True)
            P.on("vector", v_g1)
            g1_ve = ve.n
            P.set_guard(f"ps_qk{sl}", ve, g1_ve)

            P.wait("scalar", ve, g1_ve)

            def sc_g(sc):
                i = sc.activation(gcol(1), gcol(0), AF.Exp, bias=st12[:, 1:2])
            P.on("scalar", sc_g)
            g_se = se.n

            P.wait("vector", se, g_se)

            def v_g2(vv):
                vv.tensor_scalar(gcol(2), gcol(0), st12[:, 0:1], None,
                                 ALU.is_equal)
                vv.scalar_tensor_tensor(gcol(2), gcol(2), -1e30, gcol(0),
                                        ALU.mult, ALU.add)
                i = vv.reduce_max(st12[:, 2:3], gcol(2), axis=AX.X)
            P.on("vector", v_g2)
            P.wait("vector", ve, ve.n)

            def v_g3(vv):
                vv.tensor_scalar(gcol(3), gcol(0), st12[:, 2:3], None, ALU.is_ge)
                vv.tensor_mul(gcol(1), gcol(1), gcol(3))
                i = vv.reduce_sum(st12[:, 3:4], gcol(1), axis=AX.X)
            P.on("vector", v_g3)
            P.wait("vector", ve, ve.n)

            def v_g4(vv):
                i = vv.reciprocal(st12[:, 4:5], st12[:, 3:4])
            P.on("vector", v_g4)
            P.wait("vector", ve, ve.n)

            def v_g5(vv, tb=tb):
                vv.tensor_scalar(gcol(1), gcol(1), st12[:, 4:5], None, ALU.mult)
                i = vv.tensor_mul(comb_sb[:, tb * 8:(tb + 1) * 8],
                                  gcol(1), alphab_sb[:, :])
            P.on("vector", v_g5)
            P.set_guard("gate_cols", ve, ve.n)

        P.wait("scalar", pe, pe.n)

        def sc_spxn(sc):
            i = sc.dma_start(out=xnT_d.rearrange("n p s -> p n s"),
                             in_=xnT_sb[:, :].rearrange("p (n s) -> p n s",
                                                        n=NI))
            d_sp.inc(i, 16)
        P.on("scalar", sc_spxn)
        sp_all = d_sp.n

        p2_pe, p2_ve, p2_se = pe.n, ve.n, se.n
        a12.close()

        a3 = ExitStack()
        kt_sb = a3.enter_context(nc.sbuf_tensor("kt_sb", [128, 2 * S], BF16))
        vh_sb = a3.enter_context(nc.sbuf_tensor("vh_sb", [128, 2 * S], BF16))
        qh_sb = a3.enter_context(nc.sbuf_tensor("qh_sb", [128, 2 * S], BF16))
        p_sb = a3.enter_context(nc.sbuf_tensor("p_sb", [128, 8 * S], BF16))
        pT_sb = a3.enter_context(nc.sbuf_tensor("pT_sb", [128, 16 * 512], BF16))
        cte_sb = a3.enter_context(nc.sbuf_tensor("cte_sb", [128, 2 * 512], BF16))
        st3 = a3.enter_context(nc.sbuf_tensor("st3", [128, 16], F32))
        q2_sb = a3.enter_context(nc.sbuf_tensor("q2_sb", [128, S], BF16))
        bnd_sb = a3.enter_context(nc.sbuf_tensor("bnd_sb", [128, 2 * NT], F32))
        ps_S = psA
        ps_tr3 = psB[:, 0:1024].bitcast(BF16)
        ps_ctx = psB[:, 1024:2048]

        for eng in ("tensor", "vector", "scalar", "sync"):
            P.wait(eng, pe, p2_pe)
            P.wait(eng, ve, p2_ve)
            P.wait(eng, se, p2_se)
        P.wait("sync", d_sp, sp_all)
        P.wait("vector", d_mk, mask_rdy)

        pT_view = pT_sb[:, :].rearrange("p (k c) -> p k c", k=16)

        groups = [(h, g) for h in range(n_h) for g in range(4)]
        pendingB = []

        def flushB(n):
            for _ in range(min(n, len(pendingB))):
                pendingB.pop(0)()

        def plan_kv_load(h):
            ksl = h % 2
            dk = d_kA if ksl == 0 else d_kB
            kth = kt_sb[:, ksl * S:(ksl + 1) * S]
            vh = vh_sb[:, ksl * S:(ksl + 1) * S]
            P.guard_wait("sync", f"kv{ksl}")

            def s_kv(sy, h=h, kth=kth, vh=vh, dk=dk):
                i = sy.dma_start(out=kth, in_=kT_d[h, :, :])
                dk.inc(i, 16)
                i = sy.dma_start(out=vh.rearrange("p (k c) -> p k c", k=NT),
                                 in_=v_d[:, :, h * 128:(h + 1) * 128]
                                 .rearrange("k p c -> p k c"))
                dk.inc(i, 16)
            P.on("sync", s_kv)
            return dk.n

        def plan_q_load(hq):
            qsl = hq % 2
            dq = d_qA if qsl == 0 else d_qB
            qh = qh_sb[:, qsl * S:(qsl + 1) * S]
            P.guard_wait("sync", f"q{qsl}")

            def s_q(sy, hq=hq, qh=qh, dq=dq):
                i = sy.dma_start(out=qh, in_=qT_d[hq, :, :])
                dq.inc(i, 16)
            P.on("sync", s_q)
            return dq.n

        def plan_bnd(hh):
            hpar = hh % 2
            dqh = d_qA if hpar == 0 else d_qB
            qhf = qh_sb[:, hpar * S:(hpar + 1) * S]
            P.wait("vector", dqh, q_rdys[hh])
            P.guard_wait("vector", "q2buf")

            def v_q2(vv, qhf=qhf):
                i = vv.tensor_mul(q2_sb[:, :], qhf, qhf)
            P.on("vector", v_q2)
            q2_ve = ve.n
            P.wait("tensor", ve, q2_ve)
            P.guard_wait("tensor", "ps_S0")

            def t_qn(t):
                for tb in range(NT):
                    i = t.matmul(ps_S[:, tb:tb + 1],
                                 q2_sb[:, tb * 128:(tb + 1) * 128],
                                 ones_col[:, :], start=True, stop=True)
                pe.inc(i)
            P.on("tensor", t_qn)
            qn_pe = pe.n
            P.set_guard("q2buf", pe, qn_pe)
            P.wait("vector", pe, qn_pe)
            P.guard_wait("vector", f"bndbuf{hpar}")

            def v_bnd(vv, hh=hh, hpar=hpar):
                i = vv.tensor_scalar(bnd_sb[:, hpar * NT:(hpar + 1) * NT],
                                     ps_S[:, 0:NT],
                                     kn2b_sb[:, hh:hh + 1], -0.5 * SCALE,
                                     ALU.add, ALU.mult)
            P.on("vector", v_bnd)
            bnd_ve = ve.n
            P.set_guard("ps_S0", ve, bnd_ve)
            P.wait("scalar", ve, bnd_ve)

        kv_rdys = {0: plan_kv_load(0)}
        q_rdys = {0: plan_q_load(0)}
        bank_cur = 0

        for gi, (h, g) in enumerate(groups):
            par = gi % 2
            ksl = h % 2
            dk = d_kA if ksl == 0 else d_kB
            vh = vh_sb[:, ksl * S:(ksl + 1) * S]
            kth = kt_sb[:, ksl * S:(ksl + 1) * S]

            if g == 1 and h + 1 < n_h:
                kv_rdys[h + 1] = plan_kv_load(h + 1)
                q_rdys[h + 1] = plan_q_load(h + 1)
            kv_rdy = kv_rdys[h]
            q_rdy = q_rdys[h]
            qsl = h % 2
            dq = d_qA if qsl == 0 else d_qB
            qg = qh_sb[:, qsl * S + g * 512:qsl * S + (g + 1) * 512]

            if gi == 0:
                plan_bnd(0)
            if g == 2 and h + 1 < n_h:
                plan_bnd(h + 1)

            P.guard_wait("scalar", f"pbuf{par}")

            for qbi in range(4):
                qb = 4 * g + qbi
                K = (qb + 1) * 128
                nbs = (K + 511) // 512
                pq = p_sb[:, (par * 4 + qbi) * S:(par * 4 + qbi) * S + S]

                P.wait("tensor", dq, q_rdy)
                P.wait("tensor", dk, kv_rdy)

                ab = 8 + 4 * (qbi % 2)
                P.guard_wait("scalar", f"lcols{qbi % 2}")
                for nb in range(nbs):
                    n0 = nb * 512
                    n1 = min(K, n0 + 512)
                    bk = (bank_cur + nb) % 4
                    b0 = bk * 512
                    diag = (n1 == K)
                    P.guard_wait("tensor", f"ps_S{bk}")

                    def t_s(t, qbi=qbi, qb=qb, n0=n0, n1=n1, b0=b0, qg=qg,
                            kth=kth, diag=diag):
                        i = t.matmul(ps_S[:, b0:b0 + (n1 - n0)],
                                     qg[:, qbi * 128:(qbi + 1) * 128],
                                     kth[:, n0:n1], start=True, stop=not diag)
                        if diag:
                            d0 = b0 + qb * 128 - n0
                            i = t.matmul(ps_S[:, d0:d0 + 128], ident_sb[:, :],
                                         maskb_sb[:, :], start=False,
                                         stop=True)
                        pe.inc(i)
                    P.on("tensor", t_s)
                    s_pe = pe.n
                    P.wait("scalar", pe, s_pe)

                    def sc_exp(sc, pq=pq, n0=n0, n1=n1, b0=b0, qb=qb, nb=nb,
                               ab=ab):
                        i = sc.activation(pq[:, n0:n1],
                                          ps_S[:, b0:b0 + (n1 - n0)],
                                          AF.Exp,
                                          bias=bnd_sb[:, (h % 2) * NT + qb:
                                                      (h % 2) * NT + qb + 1],
                                          scale=SCALE,
                                          accum_out=st3[:, ab + nb:ab + nb + 1])
                    P.on("scalar", sc_exp)
                    P.set_guard(f"ps_S{bk}", se, se.n)
                bank_cur = (bank_cur + nbs) % 4
                if qbi == 3:
                    P.set_guard(f"q{qsl}", pe, pe.n)
                if qbi > 0:
                    flushB(1)

                exp_se = se.n
                if qbi == 3 and g == 3:
                    P.set_guard(f"bndbuf{h % 2}", se, exp_se)
                P.wait("vector", se, exp_se)

                def v_li(vv, nbs=nbs, ab=ab, qbi=qbi):
                    a = st3[:, ab:ab + 1]
                    for nb in range(1, nbs):
                        vv.tensor_add(a, a, st3[:, ab + nb:ab + nb + 1])
                    i = vv.reciprocal(st3[:, 2 + qbi % 2:3 + qbi % 2], a)
                P.on("vector", v_li)
                P.set_guard(f"lcols{qbi % 2}", ve, ve.n)
                P.wait("vector", ve, ve.n)

                def v_ps2(vv, pq=pq, K=K, qbi=qbi):
                    i = vv.tensor_scalar(pq[:, 0:K], pq[:, 0:K],
                                         st3[:, 2 + qbi % 2:3 + qbi % 2],
                                         None, ALU.mult)
                P.on("vector", v_ps2)

            pscale_ve = ve.n

            def mk_tr_piece(kts, h=h, g=g, par=par, pscale_ve=pscale_ve,
                            last=False):
                def piece():
                    P.wait("tensor", ve, pscale_ve)
                    for kt in kts:
                        q0 = max(0, kt - 4 * g)
                        cnt = 4 - q0
                        ph = kt % 2
                        P.guard_wait("tensor", f"ps_tr3{ph}")

                        def t_ptr(t, kt=kt, q0=q0, cnt=cnt, ph=ph, par=par):
                            for idx in range(cnt):
                                qbi = q0 + idx
                                sl = (par * 4 + qbi) * S
                                i = t.transpose(
                                    ps_tr3[:, ph * 1024 + idx * 128:
                                           ph * 1024 + (idx + 1) * 128],
                                    p_sb[:, sl + kt * 128:sl + (kt + 1) * 128],
                                    ident_sb[:, :])
                            pe.inc(i)
                        P.on("tensor", t_ptr)
                        ptr_pe = pe.n

                        P.wait("vector", pe, ptr_pe)

                        def v_pev(vv, kt=kt, q0=q0, cnt=cnt, ph=ph):
                            i = vv.tensor_copy(
                                pT_view[:, kt, q0 * 128:(q0 + cnt) * 128],
                                ps_tr3[:, ph * 1024:ph * 1024 + cnt * 128])
                        P.on("vector", v_pev)
                        P.set_guard(f"ps_tr3{ph}", ve, ve.n)
                    if last:
                        P.set_guard(f"pbuf{par}", pe, pe.n)
                return piece

            def mk_ctx_piece(h=h, g=g, gi=gi, vh=vh, ksl=ksl):
                def piece():
                    nkt = 4 * g + 4
                    csl = gi % 2
                    P.wait("tensor", ve, ve.n)
                    P.guard_wait("tensor", f"ps_ctx{csl}")

                    def t_ctx(t, g=g, nkt=nkt, vh=vh, csl=csl):
                        for kt in range(nkt):
                            q0 = max(0, kt - 4 * g)
                            i = t.matmul(
                                ps_ctx[:, csl * 512 + q0 * 128:csl * 512 + 512],
                                vh[:, kt * 128:(kt + 1) * 128],
                                pT_view[:, kt, q0 * 128:512],
                                start=(kt == 0), stop=(kt == nkt - 1))
                        pe.inc(i)
                    P.on("tensor", t_ctx)
                    if g == 3:
                        P.set_guard(f"kv{ksl}", pe, pe.n)
                return piece

            def mk_ev_piece(h=h, g=g, gi=gi):
                def piece():
                    csl = gi % 2
                    ctx_pe = pe.n
                    P.wait("vector", pe, ctx_pe)
                    P.wait("vector", d_sp2, d_sp2.n)

                    def v_cev(vv, csl=csl):
                        i = vv.tensor_copy(cte_sb[:, csl * 512:(csl + 1) * 512],
                                           ps_ctx[:, csl * 512:(csl + 1) * 512])
                    P.on("vector", v_cev)
                    cev_ve = ve.n
                    P.set_guard(f"ps_ctx{csl}", ve, cev_ve)

                    P.wait("gpsimd", ve, cev_ve)

                    def s_cts(gp, h=h, g=g, csl=csl):
                        i = gp.dma_start(out=ctxT_d[h, :,
                                                    g * 512:(g + 1) * 512],
                                         in_=cte_sb[:, csl * 512:
                                                    (csl + 1) * 512])
                        d_sp2.inc(i, 16)
                    P.on("gpsimd", s_cts)
                return piece

            nkt = 4 * g + 4
            half = (nkt + 1) // 2
            flushB(len(pendingB))
            pendingB = [
                mk_tr_piece(list(range(half))),
                mk_tr_piece(list(range(half, nkt)), last=True),
                mk_ctx_piece(),
                mk_ev_piece(),
            ]
            if gi == len(groups) - 1:
                flushB(4)
        p3_sp = d_sp.n
        p3_sp2 = d_sp2.n

        p3_pe, p3_ve, p3_se = pe.n, ve.n, se.n
        a3.close()

        a4 = ExitStack()
        wo_sb = a4.enter_context(nc.sbuf_tensor("wo_sb", [128, NI * E], BF16))
        ct_sb = a4.enter_context(nc.sbuf_tensor("ct_sb", [128, 2 * E], BF16))
        x4_sb = a4.enter_context(nc.sbuf_tensor("x4_sb", [128, 2 * E], F32))
        o4_sb = a4.enter_context(nc.sbuf_tensor("o4_sb", [128, 2 * E], F32))
        ps_o = [psA, psB]

        for eng in ("tensor", "vector", "scalar", "sync", "gpsimd"):
            P.wait(eng, pe, p3_pe)
            P.wait(eng, ve, p3_ve)
            P.wait(eng, se, p3_se)
        P.wait("gpsimd", d_sp, p3_sp)
        P.wait("gpsimd", d_sp2, p3_sp2)

        def g_wo(gp):
            for it in range(NI):
                i = gp.dma_start(out=wo_sb[:, it * E:(it + 1) * E],
                                 in_=wout_v[:, it, :])
                d_wv.inc(i, 16)
        P.on("gpsimd", g_wo)
        wo_rdy = d_wv.n

        P.wait("sync", d_sp, p3_sp)
        P.wait("sync", d_sp2, p3_sp2)
        for tb in range(n_p4):
            sl = tb % 2
            dc = d_cA if sl == 0 else d_cB
            dx = d_xA if sl == 0 else d_xB
            ctv = ct_sb[:, sl * E:(sl + 1) * E].rearrange(
                "p (k c) -> p k c", k=NI)
            x4 = x4_sb[:, sl * E:(sl + 1) * E]
            o4 = o4_sb[:, sl * E:(sl + 1) * E]
            ps = ps_o[sl]

            P.guard_wait("sync", f"ct{sl}")

            def s_ct(sy, tb=tb, ctv=ctv, dc=dc):
                i = sy.dma_start(out=ctv,
                                 in_=ctxT_d[:, :, tb * 128:(tb + 1) * 128]
                                 .rearrange("k p c -> p k c"))
                dc.inc(i, 16)
            P.on("sync", s_ct)
            ct_rdy = dc.n

            P.guard_wait("sync", f"x4{sl}")

            def s_x4(sy, tb=tb, x4=x4, dx=dx):
                i = sy.dma_start(out=x4, in_=x_v[tb, :, :])
                dx.inc(i, 16)
            P.on("sync", s_x4)
            x4_rdy = dx.n

            P.wait("tensor", dc, ct_rdy)
            if tb < 2:
                P.wait("tensor", d_wv, wo_rdy)
            P.guard_wait("tensor", f"ps_o{sl}")

            def t_o(t, ctv=ctv, ps=ps):
                for it in range(NI):
                    lhsT = ctv[:, it, :]
                    for nb in range(4):
                        t.matmul(ps[:, nb * 512:(nb + 1) * 512], lhsT,
                                 wo_sb[:, it * E + nb * 512:
                                       it * E + (nb + 1) * 512],
                                 start=(it == 0), stop=False)
                for nb in range(4):
                    i = t.matmul(ps[:, nb * 512:(nb + 1) * 512],
                                 ones_row[0:1, 0:128],
                                 bout_row[0:1, nb * 512:(nb + 1) * 512],
                                 start=False, stop=True)
                pe.inc(i)
            P.on("tensor", t_o)
            o_pe = pe.n
            P.set_guard(f"ct{sl}", pe, o_pe)

            P.wait("vector", pe, o_pe)
            P.wait("vector", dx, x4_rdy)
            P.wait("vector", d_out, d_out.n)

            def v_o(vv, ps=ps, x4=x4, o4=o4):
                i = vv.tensor_add(o4, ps[:, :], x4)
            P.on("vector", v_o)
            o_ve = ve.n
            P.set_guard(f"ps_o{sl}", ve, o_ve)
            P.set_guard(f"x4{sl}", ve, o_ve)

            P.wait("sync", ve, o_ve)

            def s_o(sy, tb=tb, o4=o4):
                i = sy.dma_start(out=out_v[tb, :, :], in_=o4)
                d_out.inc(i, 16)
            P.on("sync", s_o)
        p4_out = d_out.n

        p4_pe, p4_ve, p4_se = pe.n, ve.n, se.n
        a4.close()

        a5 = ExitStack()
        xnc_sb = a5.enter_context(nc.sbuf_tensor("xnc_sb", [128, NI * CHUNK], BF16))
        acc_sb = a5.enter_context(nc.sbuf_tensor("acc_sb", [128, 8 * E], F32))
        hT_sb = a5.enter_context(nc.sbuf_tensor("hT_sb", [128, NI * CHUNK], BF16))
        f2_sb = a5.enter_context(nc.sbuf_tensor("f2_sb", [128, 8 * E], BF16))
        f1_sb = a5.enter_context(nc.sbuf_tensor("f1_sb", [128, 4 * E], BF16))
        b1_row = a5.enter_context(nc.sbuf_tensor("b1_row", [1, E], BF16))
        b2_row = a5.enter_context(nc.sbuf_tensor("b2_row", [1, E], BF16))
        ps_h = psA
        ps_f = psB

        for eng in ("tensor", "vector", "scalar", "sync", "gpsimd"):
            P.wait(eng, pe, p4_pe)
            P.wait(eng, ve, p4_ve)
            P.wait(eng, se, p4_se)

        out_ch = out.rearrange("(tb p) j -> tb p j", p=128)

        for c in range(n_ch):
            P.guard_wait("sync", "xnc")
            P.wait("sync", d_sp, d_sp.n)

            def s_xnc(sy, c=c):
                i = sy.dma_start(
                    out=xnc_sb[:, :].rearrange("p (n s) -> p n s", n=NI),
                    in_=xnT_d[:, :, c * CHUNK:(c + 1) * CHUNK]
                    .rearrange("n p s -> p n s"))
                d_mc.inc(i, 16)
            P.on("sync", s_xnc)
            xnc_rdy = d_mc.n

            P.wait("sync", d_out, d_out.n)

            def s_acc(sy, c=c):
                i = sy.dma_start(
                    out=acc_sb[:, :].rearrange("p (tb j) -> p tb j", tb=8),
                    in_=out_ch[c * 8:(c + 1) * 8, :, :]
                    .rearrange("t p j -> p t j"))
                d_mc.inc(i, 16)
            P.on("sync", s_acc)
            acc_rdy = d_mc.n
            xnc_rdy = d_mc.n
            P.wait("vector", d_mc, acc_rdy)

            for e in range(NEXP):
                P.guard_wait("gpsimd", "b12")

                def g_b(gp, e=e):
                    i = gp.dma_start(out=b1_row[:, :], in_=fc1_b[e, :][None, :])
                    d_gb.inc(i, 16)
                    i = gp.dma_start(out=b2_row[:, :], in_=fc2_b[e, :][None, :])
                    d_gb.inc(i, 16)
                P.on("gpsimd", g_b)
                b_rdy = d_gb.n

                f1_rdys = {}

                def plan_f1_fill(ib, e=e):
                    sl = ib % 4
                    df = (d_f1A, d_f1B, d_f1C, d_f1D)[sl]
                    f1 = f1_sb[:, sl * E:(sl + 1) * E]
                    P.guard_wait("gpsimd", f"f1{sl}")

                    def g_f1(gp, e=e, ib=ib, f1=f1, df=df):
                        i = gp.dma_start(
                            out=f1.rearrange("p (dt j) -> p dt j", dt=NI),
                            in_=fc1_v[e, :, :, ib * 128:(ib + 1) * 128])
                        df.inc(i, 16)
                    P.on("gpsimd", g_f1)
                    f1_rdys[ib] = df.n

                def plan_f2_tile(half, it8, dfh, e=e):
                    if it8 == 0:
                        P.guard_wait("gpsimd", "f2buf")

                    def g_f2(gp, e=e, half=half, it8=it8, dfh=dfh):
                        i = gp.dma_start(
                            out=f2_sb[:, it8 * E:(it8 + 1) * E],
                            in_=fc2_v[e, :, half * 8 + it8, :])
                        dfh.inc(i, 16)
                    P.on("gpsimd", g_f2)
                    return dfh.n

                def plan_fc1(ib, e=e):
                    sl = ib % 4
                    df = (d_f1A, d_f1B, d_f1C, d_f1D)[sl]
                    f1 = f1_sb[:, sl * E:(sl + 1) * E]

                    P.wait("tensor", df, f1_rdys[ib])
                    if e == 0 and ib < 2:
                        P.wait("tensor", d_mc, xnc_rdy)
                    if ib == 0:
                        P.wait("tensor", d_gb, b_rdy)
                    psl = ib % 2
                    P.guard_wait("tensor", f"ps_h{psl}")

                    def t_f1(t, ib=ib, f1=f1, psl=psl):
                        psh = ps_h[:, psl * 1024:(psl + 1) * 1024]
                        for dt in range(NI):
                            lhsT = f1[:, dt * 128:(dt + 1) * 128]
                            for nb in range(2):
                                t.matmul(psh[:, nb * 512:(nb + 1) * 512], lhsT,
                                         xnc_sb[:, dt * CHUNK + nb * 512:
                                                dt * CHUNK + (nb + 1) * 512],
                                         start=(dt == 0), stop=False)
                        for nb in range(2):
                            i = t.matmul(psh[:, nb * 512:(nb + 1) * 512],
                                         b1_row[0:1, ib * 128:(ib + 1) * 128],
                                         ones_row[0:1, nb * 512:(nb + 1) * 512],
                                         start=False, stop=True)
                        pe.inc(i)
                    P.on("tensor", t_f1)
                    f1_pe = pe.n
                    P.set_guard(f"f1{sl}", pe, f1_pe)

                    P.wait("scalar", pe, f1_pe)
                    P.guard_wait("scalar", "hT_lo" if ib < 8 else "hT_hi")

                    def sc_gel(sc, ib=ib, psl=psl):
                        i = sc.activation(hT_sb[:, ib * CHUNK:(ib + 1) * CHUNK],
                                          ps_h[:, psl * 1024:(psl + 1) * 1024],
                                          AF.Gelu)
                    P.on("scalar", sc_gel)
                    P.set_guard(f"ps_h{psl}", se, se.n)

                def plan_fc2(half, f2_rdy, gel_se, e=e, c=c):
                    dfh = d_f2A if half == 0 else d_f2B
                    P.wait("tensor", dfh, f2_rdy)
                    P.wait("tensor", se, gel_se)
                    for tb in range(8):
                        for jh in range(2):
                            P.guard_wait("tensor", f"ps_f{jh}")

                            def t_f2(t, tb=tb, half=half, jh=jh):
                                psf = ps_f[:, jh * 1024:(jh + 1) * 1024]
                                last = None
                                for it8 in range(8):
                                    it = half * 8 + it8
                                    lhsT = hT_sb[:, it * CHUNK + tb * 128:
                                                 it * CHUNK + (tb + 1) * 128]
                                    for nb in range(2):
                                        j0 = jh * 1024 + nb * 512
                                        stop = (half == 1 and it8 == 7)
                                        last = t.matmul(
                                            psf[:, nb * 512:(nb + 1) * 512],
                                            lhsT,
                                            f2_sb[:, it8 * E + j0:
                                                  it8 * E + j0 + 512],
                                            start=(it8 == 0), stop=stop)
                                if half == 0:
                                    for nb in range(2):
                                        j0 = jh * 1024 + nb * 512
                                        last = t.matmul(
                                            psf[:, nb * 512:(nb + 1) * 512],
                                            ones_row[0:1, 0:128],
                                            b2_row[0:1, j0:j0 + 512],
                                            start=False, stop=True)
                                pe.inc(last)
                            P.on("tensor", t_f2)
                            f2_pe = pe.n

                            P.wait("vector", pe, f2_pe)

                            def v_acc(vv, c=c, e=e, tb=tb, jh=jh):
                                cidx = (c * 8 + tb) * 8 + e
                                csl = comb_sb[:, cidx:cidx + 1]
                                at = acc_sb[:, tb * E + jh * 1024:
                                            tb * E + (jh + 1) * 1024]
                                i = vv.scalar_tensor_tensor(
                                    at, ps_f[:, jh * 1024:(jh + 1) * 1024],
                                    csl, at, ALU.mult, ALU.add)
                            P.on("vector", v_acc)
                            P.set_guard(f"ps_f{jh}", ve, ve.n)
                    P.set_guard("f2buf", pe, pe.n)
                    P.set_guard("hT_lo" if half == 0 else "hT_hi", pe, pe.n)

                for ib in range(4):
                    plan_f1_fill(ib)
                f2_rdy0 = plan_f2_tile(0, 0, d_f2A)
                for k in range(8):
                    plan_fc1(k)
                    if k < 4:
                        plan_f1_fill(k + 4)
                    if k < 7:
                        f2_rdy0 = plan_f2_tile(0, k + 1, d_f2A)
                gel_lo = se.n
                plan_fc2(0, f2_rdy0, gel_lo)

                for ib in range(8, 12):
                    plan_f1_fill(ib)
                f2_rdy1 = plan_f2_tile(1, 0, d_f2B)
                for k in range(8, NI):
                    plan_fc1(k)
                    if k < 12:
                        plan_f1_fill(k + 4)
                    if k < 15:
                        f2_rdy1 = plan_f2_tile(1, k - 7, d_f2B)
                gel_hi = se.n
                P.set_guard("b12", pe, pe.n)
                plan_fc2(1, f2_rdy1, gel_hi)

            acc_ve = ve.n
            P.wait("sync", ve, acc_ve)

            def s_fin(sy, c=c):
                i = sy.dma_start(out=out_ch[c * 8:(c + 1) * 8, :, :]
                                 .rearrange("t p j -> p t j"),
                                 in_=acc_sb[:, :].rearrange(
                                     "p (tb j) -> p tb j", tb=8))
                d_out.inc(i, 16)
            P.on("sync", s_fin)
            P.set_guard("xnc", pe, pe.n)

        a5.close()

        P.wait("sync", d_out, d_out.n)
        P.wait("sync", d_sp, d_sp.n)

        with nc.Block() as block:
            @block.tensor
            def _(t):
                for fn in P.q["tensor"]:
                    fn(t)

            @block.vector
            def _(v):
                for fn in P.q["vector"]:
                    fn(v)

            @block.scalar
            def _(s):
                for fn in P.q["scalar"]:
                    fn(s)

            @block.sync
            def _(s):
                for fn in P.q["sync"]:
                    fn(s)

            @block.gpsimd
            def _(g):
                for fn in P.q["gpsimd"]:
                    fn(g)

    return nc


_MASK = None
_IDENT = None


def _consts():
    global _MASK, _IDENT
    if _MASK is None:
        m = np.zeros((128, 128), dtype=np.float32)
        iu = np.triu_indices(128, k=1)
        m[iu] = NEG
        _MASK = m
        _IDENT = np.eye(128, dtype=np.float32)
    return _MASK, _IDENT


_NC_CACHE = None


def kernel(L0_attn_inX, L0_attn_W_qkv, L0_attn_b_qkv, L0_attn_W_out,
           L0_attn_b_out, L0_ffn_gate_w, L0_ffn_fc1_w, L0_ffn_fc1_b,
           L0_ffn_fc2_w, L0_ffn_fc2_b, L0_ffn_alpha):
    global _NC_CACHE
    x = np.ascontiguousarray(np.asarray(L0_attn_inX, dtype=np.float32))
    mask, ident = _consts()
    shared = {
        "w_qkv": np.ascontiguousarray(np.asarray(L0_attn_W_qkv, np.float32)),
        "b_qkv": np.ascontiguousarray(np.asarray(L0_attn_b_qkv, np.float32)),
        "w_out": np.ascontiguousarray(np.asarray(L0_attn_W_out, np.float32)),
        "b_out": np.ascontiguousarray(np.asarray(L0_attn_b_out, np.float32)),
        "gate_w": np.ascontiguousarray(np.asarray(L0_ffn_gate_w, np.float32)),
        "fc1_w": np.ascontiguousarray(np.asarray(L0_ffn_fc1_w, np.float32)),
        "fc1_b": np.ascontiguousarray(np.asarray(L0_ffn_fc1_b, np.float32)),
        "fc2_w": np.ascontiguousarray(np.asarray(L0_ffn_fc2_w, np.float32)),
        "fc2_b": np.ascontiguousarray(np.asarray(L0_ffn_fc2_b, np.float32)),
        "alpha": np.ascontiguousarray(np.asarray(L0_ffn_alpha, np.float32)),
        "cmask": mask, "ident": ident,
    }
    if _NC_CACHE is None:
        _NC_CACHE = build()
    nc = _NC_CACHE
    in_maps = [dict(shared, x=np.ascontiguousarray(x[b])) for b in range(8)]
    res = run_bass_kernel_spmd(nc, in_maps, core_ids=list(range(8)))
    return np.stack([r["out"] for r in res.results], axis=0)


# version 22
# speedup vs baseline: 1.2157x; 1.0041x over previous
import numpy as np
from contextlib import ExitStack

import concourse.bass as bass
import concourse.mybir as mybir
from concourse.bass_utils import run_bass_kernel_spmd

F32 = mybir.dt.float32
BF16 = mybir.dt.bfloat16
AF = mybir.ActivationFunctionType
ALU = mybir.AluOpType
AX = mybir.AxisListType

E = 2048
S = 2048
H = 16
HD = 128
NT = S // 128
NI = E // 128
NEXP = 8
SCALE = 1.0 / float(np.sqrt(HD))
CHUNK = 1024
NCH = S // CHUNK
EPS = 1e-5
NEG = -30000.0


class Sem:
    def __init__(self, ctx, nc, name):
        self.h = ctx.enter_context(nc.semaphore(name))
        self.n = 0

    def inc(self, instr, k=1):
        instr.then_inc(self.h, k)
        self.n += k
        return self.n


class RecInstr:

    def __init__(self):
        self.incs = []

    def then_inc(self, sem, k):
        self.incs.append((sem, k))
        return self


class Recorder:

    def __init__(self):
        self.calls = []

    def __getattr__(self, name):
        def call(*a, **kw):
            ri = RecInstr()
            self.calls.append((name, a, kw, ri))
            return ri
        return call


class Plan:
    ENGS = ("tensor", "vector", "scalar", "sync", "gpsimd")

    def __init__(self):
        self.q = {k: [] for k in self.ENGS}
        self.guards = {}
        self.auto = {}

    def set_auto(self, eng, sem):
        self.auto[eng] = sem

    def on(self, eng, fn):
        rec = Recorder()
        fn(rec)
        auto = self.auto.get(eng)
        base = None
        if auto is not None:
            base = auto.n
            for name, a, kw, ri in rec.calls:
                ri.incs = [(s, k) for (s, k) in ri.incs if s is not auto.h]
                if name not in ("wait_ge", "dma_start"):
                    auto.n += 1

        def replay(engine, calls=rec.calls, auto=auto, base=base):
            k = base
            for name, a, kw, ri in calls:
                if auto is not None and name != "wait_ge" and k > 0:
                    engine.wait_ge(auto.h, k)
                instr = getattr(engine, name)(*a, **kw)
                for sem, kk in ri.incs:
                    instr.then_inc(sem, kk)
                if auto is not None and name not in ("wait_ge", "dma_start"):
                    instr.then_inc(auto.h, 1)
                    k += 1
        self.q[eng].append(replay)

    def wait(self, eng, sem, val):
        if val > 0:
            self.q[eng].append(lambda e, s=sem.h, v=val: e.wait_ge(s, v))

    def guard_wait(self, eng, slot):
        g = self.guards.get(slot)
        if g is not None:
            self.wait(eng, g[0], g[1])

    def set_guard(self, slot, sem, val):
        self.guards[slot] = (sem, val)


def build(debug=False, n_tt=NT, n_jb=2 * H, n_vtb=NT, n_gtb=NT, n_h=H,
          n_p4=NT, n_ch=NCH):
    nc = bass.Bass("TRN2", target_bir_lowering=False)

    x_in = nc.dram_tensor("x", [S, E], F32, kind="ExternalInput").ap()
    w_qkv = nc.dram_tensor("w_qkv", [E, 3 * E], F32, kind="ExternalInput").ap()
    b_qkv = nc.dram_tensor("b_qkv", [3 * E], F32, kind="ExternalInput").ap()
    w_out = nc.dram_tensor("w_out", [E, E], F32, kind="ExternalInput").ap()
    b_out = nc.dram_tensor("b_out", [E], F32, kind="ExternalInput").ap()
    gate_w = nc.dram_tensor("gate_w", [E, NEXP], F32, kind="ExternalInput").ap()
    fc1_w = nc.dram_tensor("fc1_w", [NEXP, E, E], F32, kind="ExternalInput").ap()
    fc1_b = nc.dram_tensor("fc1_b", [NEXP, E], F32, kind="ExternalInput").ap()
    fc2_w = nc.dram_tensor("fc2_w", [NEXP, E, E], F32, kind="ExternalInput").ap()
    fc2_b = nc.dram_tensor("fc2_b", [NEXP, E], F32, kind="ExternalInput").ap()
    alpha = nc.dram_tensor("alpha", [NEXP], F32, kind="ExternalInput").ap()
    cmask = nc.dram_tensor("cmask", [128, 128], F32, kind="ExternalInput").ap()
    ident_in = nc.dram_tensor("ident", [128, 128], F32, kind="ExternalInput").ap()
    out = nc.dram_tensor("out", [S, E], F32, kind="ExternalOutput").ap()

    kind_dbg = "ExternalOutput" if debug else "Internal"
    xnT_d = nc.dram_tensor("xnT_d", [NI, 128, S], BF16, kind=kind_dbg).ap()
    qT_d = nc.dram_tensor("qT_d", [H, 128, S], BF16, kind=kind_dbg).ap()
    kT_d = nc.dram_tensor("kT_d", [H, 128, S], BF16, kind=kind_dbg).ap()
    v_d = nc.dram_tensor("v_d", [NT, 128, E], BF16, kind=kind_dbg).ap()
    ctxT_d = nc.dram_tensor("ctxT_d", [H, 128, S], BF16, kind=kind_dbg).ap()

    wqkv_v = w_qkv.rearrange("(it p) j -> p it j", p=128)
    wout_v = w_out.rearrange("(it p) j -> p it j", p=128)
    gate_v = gate_w.rearrange("(it p) e -> p it e", p=128)
    fc1_v = fc1_w.rearrange("e (dt p) i -> e p dt i", p=128)
    fc2_v = fc2_w.rearrange("e (it p) j -> e p it j", p=128)
    x_v = x_in.rearrange("(tt p) d -> tt p d", p=128)
    out_v = out.rearrange("(tt p) d -> tt p d", p=128)

    P = Plan()

    with ExitStack() as ctx:
        En = ctx.enter_context

        pe = Sem(ctx, nc, "pe")
        ve = Sem(ctx, nc, "ve")
        se = Sem(ctx, nc, "se")
        d_misc = Sem(ctx, nc, "d_misc")
        d_xA = Sem(ctx, nc, "d_xA")
        d_xB = Sem(ctx, nc, "d_xB")
        d_wA = Sem(ctx, nc, "d_wA")
        d_wB = Sem(ctx, nc, "d_wB")
        d_wv = Sem(ctx, nc, "d_wv")
        d_sp = Sem(ctx, nc, "d_sp")
        d_kA = Sem(ctx, nc, "d_kA")
        d_kB = Sem(ctx, nc, "d_kB")
        d_qA = Sem(ctx, nc, "d_qA")
        d_qB = Sem(ctx, nc, "d_qB")
        d_cA = Sem(ctx, nc, "d_cA")
        d_cB = Sem(ctx, nc, "d_cB")
        d_out = Sem(ctx, nc, "d_out")
        d_f1A = Sem(ctx, nc, "d_f1A")
        d_f1B = Sem(ctx, nc, "d_f1B")
        d_f1C = Sem(ctx, nc, "d_f1C")
        d_f1D = Sem(ctx, nc, "d_f1D")
        d_f2A = Sem(ctx, nc, "d_f2A")
        d_f2B = Sem(ctx, nc, "d_f2B")
        d_mc = Sem(ctx, nc, "d_mc")
        d_gb = Sem(ctx, nc, "d_gb")
        d_mk = Sem(ctx, nc, "d_mk")
        d_sp2 = Sem(ctx, nc, "d_sp2")
        pe2 = Sem(ctx, nc, "pe2")
        P.set_auto("vector", ve)
        P.set_auto("scalar", se)

        mask_sb = En(nc.sbuf_tensor("mask_sb", [128, 128], F32))
        maskb_sb = En(nc.sbuf_tensor("maskb_sb", [128, 128], BF16))
        ident_sb = En(nc.sbuf_tensor("ident_sb", [128, 128], BF16))
        bout_row = En(nc.sbuf_tensor("bout_row", [1, E], BF16))
        alpha_row = En(nc.sbuf_tensor("alpha_row", [1, NEXP], BF16))
        ones_row = En(nc.sbuf_tensor("ones_row", [1, S], BF16))
        gate_sb = En(nc.sbuf_tensor("gate_sb", [128, NI * NEXP], BF16))
        comb_sb = En(nc.sbuf_tensor("comb_sb", [128, NT * NEXP], F32))
        alphab_sb = En(nc.sbuf_tensor("alphab_sb", [128, NEXP], F32))
        ones_col = En(nc.sbuf_tensor("ones_col", [128, 1], BF16))
        kn2b_sb = En(nc.sbuf_tensor("kn2b_sb", [128, H], F32))
        psA = En(nc.psum_tensor("psA", [128, S], F32))
        psB = En(nc.psum_tensor("psB", [128, S], F32))

        def g0(gp):
            i = gp.dma_start(out=ident_sb[:, :], in_=ident_in)
            d_misc.inc(i, 16)
            i = gp.dma_start(out=maskb_sb[:, :], in_=cmask)
            d_misc.inc(i, 16)
            i = gp.dma_start(out=bout_row[:, :], in_=b_out[None, :])
            d_misc.inc(i, 16)
            i = gp.dma_start(out=alpha_row[:, :], in_=alpha[None, :])
            d_misc.inc(i, 16)
            for it in range(NI):
                i = gp.dma_start(out=gate_sb[:, it * 8:(it + 1) * 8],
                                 in_=gate_v[:, it, :])
                d_misc.inc(i, 16)
        P.on("gpsimd", g0)

        def s0(sy):
            i = sy.dma_start(out=mask_sb[:, :], in_=cmask)
            d_mk.inc(i, 16)
        P.on("sync", s0)
        mask_rdy = d_mk.n

        def v0(vv):
            vv.memset(ones_col[:, :], 1.0)
            i = vv.memset(ones_row[:, :], 1.0)
        P.on("vector", v0)
        ones_ve = ve.n

        a12 = ExitStack()
        xnT_sb = a12.enter_context(nc.sbuf_tensor("xnT_sb", [128, NI * S], BF16))
        wv_sb = a12.enter_context(nc.sbuf_tensor("wv_sb", [128, NI * E], BF16))
        bqkv_row = a12.enter_context(nc.sbuf_tensor("bqkv_row", [1, 3 * E], BF16))
        xt_sb = a12.enter_context(nc.sbuf_tensor("xt_sb", [128, 2 * E], F32))
        sq_sb = a12.enter_context(nc.sbuf_tensor("sq_sb", [128, E], F32))
        xnt_sb = a12.enter_context(nc.sbuf_tensor("xnt_sb", [128, 2 * E], BF16))
        wqk_sb = a12.enter_context(nc.sbuf_tensor("wqk_sb", [128, 2 * E], BF16))
        ev_sb = a12.enter_context(nc.sbuf_tensor("ev_sb", [128, 2 * E], BF16))
        st12 = a12.enter_context(nc.sbuf_tensor("st12", [128, 16], F32))
        eg_sb = a12.enter_context(nc.sbuf_tensor("eg_sb", [128, 6 * NEXP], F32))
        k2_sb = a12.enter_context(nc.sbuf_tensor("k2_sb", [128, S], BF16))
        knm_sb = a12.enter_context(nc.sbuf_tensor("knm_sb", [1, 1], BF16))

        def g_bq(gp):
            i = gp.dma_start(out=bqkv_row[:, :], in_=b_qkv[None, :])
            d_misc.inc(i, 16)
        P.on("gpsimd", g_bq)
        misc_n = d_misc.n

        def v_z(vv):
            vv.memset(st12[:, :], 0.0)
        P.on("vector", v_z)
        stz_ve = ve.n
        if debug:
            def v_z2(vv):
                vv.memset(xnT_sb[:, :], 0.0)
                vv.memset(ev_sb[:, :], 0.0)
                vv.memset(xnt_sb[:, :], 0.0)
            P.on("vector", v_z2)

        ps_tr = psA[:, 0:1024].bitcast(BF16)

        xnT_view = xnT_sb[:, :].rearrange("p (dt s) -> p dt s", dt=NI)
        for tt in range(n_tt):
            sl = tt % 2
            dx = d_xA if sl == 0 else d_xB
            xt = xt_sb[:, sl * E:(sl + 1) * E]
            xnt = xnt_sb[:, sl * E:(sl + 1) * E]
            c0 = sl * 8
            st = lambda k, c0=c0: st12[:, c0 + k:c0 + k + 1]

            P.guard_wait("sync", f"xt{sl}")

            def s_ld(sy, tt=tt, dx=dx, xt=xt):
                i = sy.dma_start(out=xt, in_=x_v[tt, :, :])
                dx.inc(i, 16)
            P.on("sync", s_ld)
            x_rdy = dx.n

            P.wait("scalar", dx, x_rdy)
            if tt < 2:
                P.wait("scalar", ve, stz_ve)

            def sc_sq(sc, xt=xt, st=st):
                i = sc.activation(sq_sb[:, :], xt, AF.Square, accum_out=st(2))
            P.on("scalar", sc_sq)
            ssq_se = se.n

            P.wait("vector", dx, x_rdy)
            P.wait("vector", se, ssq_se)

            def v_stats(vv, xt=xt, st=st):
                vv.reduce_sum(st(0), xt, axis=AX.X)
                vv.tensor_scalar(st(1), st(0), 1.0 / E, None, ALU.mult)
                vv.tensor_scalar(st(3), st(2), 1.0 / E, None, ALU.mult)
                vv.tensor_mul(st(4), st(1), st(1))
                vv.tensor_sub(st(4), st(3), st(4))
                vv.tensor_scalar(st(4), st(4), EPS, None, ALU.add)
                i = vv.reciprocal(st(5), st(4))
            P.on("vector", v_stats)
            rv_ve = ve.n

            P.wait("scalar", ve, rv_ve)

            def sc_rstd(sc, st=st):
                i = sc.activation(st(6), st(5), AF.Sqrt)
            P.on("scalar", sc_rstd)
            rstd_se = se.n

            P.wait("vector", se, rstd_se)

            def v_nms(vv, st=st):
                i = vv.scalar_tensor_tensor(st(7), st(1), -1.0, st(6),
                                            ALU.mult, ALU.mult)
            P.on("vector", v_nms)
            nms_ve = ve.n

            P.wait("vector", ve, nms_ve)

            def v_xn(vv, xt=xt, xnt=xnt, st=st):
                i = vv.tensor_scalar(xnt, xt, st(6), st(7), ALU.mult, ALU.add)
            P.on("vector", v_xn)
            xn_ve = ve.n
            P.set_guard(f"xt{sl}", ve, xn_ve)

            P.wait("tensor", ve, xn_ve)
            if tt == 0:
                P.wait("tensor", d_misc, misc_n)
            for grp in range(4):
                ph = grp % 2
                P.guard_wait("tensor", f"ps_tr{ph}")

                def t_tr(t, grp=grp, ph=ph, xnt=xnt):
                    for k in range(4):
                        i = t.transpose(ps_tr[:, ph * 1024 + k * 128:
                                              ph * 1024 + (k + 1) * 128],
                                        xnt[:, (grp * 4 + k) * 128:
                                            (grp * 4 + k + 1) * 128],
                                        ident_sb[:, :])
                    pe.inc(i)
                P.on("tensor", t_tr)
                tr_pe = pe.n

                P.wait("vector", pe, tr_pe)

                def v_ev(vv, grp=grp, ph=ph, tt=tt):
                    src = ps_tr[:, ph * 1024:ph * 1024 + 512]
                    src = src.rearrange("p (k c) -> p k c", k=4)
                    dst = xnT_view[:, grp * 4:grp * 4 + 4,
                                   tt * 128:(tt + 1) * 128]
                    i = vv.tensor_copy(dst, src)
                P.on("vector", v_ev)
                P.set_guard(f"ps_tr{ph}", ve, ve.n)
        xnT_ve = ve.n
        if debug:
            dbg_xnt = nc.dram_tensor("dbg_xnt", [128, E], BF16,
                                     kind="ExternalOutput").ap()
            dbg_st = nc.dram_tensor("dbg_st", [128, 16], F32,
                                    kind="ExternalOutput").ap()
            P.wait("sync", ve, xnT_ve)

            def s_dbg1(sy):
                i = sy.dma_start(out=dbg_xnt, in_=xnt_sb[:, 0:E])
                d_sp.inc(i, 16)
                i = sy.dma_start(out=dbg_st, in_=st12[:, :])
                d_sp.inc(i, 16)
            P.on("sync", s_dbg1)
        ps_qk = [psA, psB]

        for jb in range(n_jb):
            sl = jb % 2
            dw = d_wA if sl == 0 else d_wB
            wsl = wqk_sb[:, sl * E:(sl + 1) * E]
            esl = ev_sb[:, sl * E:(sl + 1) * E]
            ps = ps_qk[sl]
            j0 = jb * 128

            P.guard_wait("gpsimd", f"wqk{sl}")

            def g_w(gp, wsl=wsl, dw=dw, j0=j0):
                i = gp.dma_start(out=wsl.rearrange("p (it j) -> p it j", it=NI),
                                 in_=wqkv_v[:, :, j0:j0 + 128])
                dw.inc(i, 16)
            P.on("gpsimd", g_w)
            w_rdy = dw.n

            P.wait("tensor", dw, w_rdy)
            if jb < 2:
                P.wait("tensor", ve, xnT_ve)
                P.wait("tensor", ve, ones_ve)
            P.guard_wait("tensor", f"ps_qk{sl}")

            def t_qk(t, wsl=wsl, ps=ps, j0=j0):
                for it in range(NI):
                    lhsT = wsl[:, it * 128:(it + 1) * 128]
                    for nb in range(4):
                        t.matmul(ps[:, nb * 512:(nb + 1) * 512], lhsT,
                                 xnT_sb[:, it * S + nb * 512:
                                        it * S + (nb + 1) * 512],
                                 start=(it == 0), stop=False)
                for nb in range(4):
                    i = t.matmul(ps[:, nb * 512:(nb + 1) * 512],
                                 bqkv_row[0:1, j0:j0 + 128],
                                 ones_row[0:1, nb * 512:(nb + 1) * 512],
                                 start=False, stop=True)
                pe.inc(i)
            P.on("tensor", t_qk)
            qk_pe = pe.n
            P.set_guard(f"wqk{sl}", pe, qk_pe)

            P.wait("scalar", pe, qk_pe)
            P.wait("scalar", d_sp, d_sp.n)

            def sc_ev(sc, ps=ps, esl=esl, jb=jb):
                i = sc.activation(esl, ps[:, :], AF.Copy)
                dst = qT_d[jb, :, :] if jb < H else kT_d[jb - H, :, :]
                i = sc.dma_start(out=dst, in_=esl)
                d_sp.inc(i, 16)
            P.on("scalar", sc_ev)
            evac_se = se.n
            if jb >= H:
                hh = jb - H
                P.wait("vector", se, evac_se)

                def v_k2(vv, esl=esl):
                    i = vv.tensor_mul(k2_sb[:, :], esl, esl)
                P.on("vector", v_k2)
                k2_ve = ve.n
                P.wait("tensor", ve, k2_ve)

                def t_kn(t, ps=ps):
                    for nb in range(4):
                        i = t.matmul(ps[0:1, nb * 512:(nb + 1) * 512],
                                     ones_col[:, :],
                                     k2_sb[:, nb * 512:(nb + 1) * 512],
                                     start=True, stop=True)
                    pe.inc(i)
                P.on("tensor", t_kn)
                kn_pe = pe.n
                P.wait("vector", pe, kn_pe)

                def v_knm(vv, ps=ps):
                    i = vv.reduce_max(knm_sb[0:1, 0:1], ps[0:1, 0:S], axis=AX.X)
                P.on("vector", v_knm)
                knm_ve = ve.n
                P.wait("tensor", ve, knm_ve)

                def t_knb(t, ps=ps):
                    i = t.matmul(ps[:, 0:1], ones_row[0:1, 0:128],
                                 knm_sb[0:1, 0:1], start=True, stop=True)
                    pe.inc(i)
                P.on("tensor", t_knb)
                knb_pe = pe.n
                P.wait("vector", pe, knb_pe)

                def v_knb(vv, ps=ps, hh=hh):
                    i = vv.tensor_copy(kn2b_sb[:, hh:hh + 1], ps[:, 0:1])
                P.on("vector", v_knb)
                P.set_guard(f"ps_qk{sl}", ve, ve.n)
            else:
                P.set_guard(f"ps_qk{sl}", se, evac_se)

        def g_wv(gp):
            for it in range(NI):
                i = gp.dma_start(out=wv_sb[:, it * E:(it + 1) * E],
                                 in_=wqkv_v[:, it, 2 * E:3 * E])
                d_wv.inc(i, 16)
        P.on("gpsimd", g_wv)
        wv_rdy = d_wv.n

        for tb in range(n_vtb):
            sl = tb % 2
            ps = ps_qk[sl]
            esl = ev_sb[:, sl * E:(sl + 1) * E]

            if tb < 2:
                P.wait("tensor", d_wv, wv_rdy)
            P.guard_wait("tensor", f"ps_qk{sl}")

            def t_v(t, tb=tb, ps=ps):
                for it in range(NI):
                    lhsT = xnT_sb[:, it * S + tb * 128:it * S + (tb + 1) * 128]
                    for nb in range(4):
                        t.matmul(ps[:, nb * 512:(nb + 1) * 512], lhsT,
                                 wv_sb[:, it * E + nb * 512:
                                       it * E + (nb + 1) * 512],
                                 start=(it == 0), stop=False)
                for nb in range(4):
                    i = t.matmul(ps[:, nb * 512:(nb + 1) * 512],
                                 ones_row[0:1, 0:128],
                                 bqkv_row[0:1, 2 * E + nb * 512:
                                          2 * E + (nb + 1) * 512],
                                 start=False, stop=True)
                pe.inc(i)
            P.on("tensor", t_v)
            v_pe = pe.n

            P.wait("scalar", pe, v_pe)
            P.wait("scalar", d_sp, d_sp.n)

            def sc_vev(sc, ps=ps, esl=esl, tb=tb):
                i = sc.activation(esl, ps[:, :], AF.Copy)
                i = sc.dma_start(out=v_d[tb, :, :], in_=esl)
                d_sp.inc(i, 16)
            P.on("scalar", sc_vev)
            P.set_guard(f"ps_qk{sl}", se, se.n)

        P.guard_wait("tensor", "ps_qk0")
        P.wait("tensor", d_misc, misc_n)
        P.wait("tensor", ve, xnT_ve)

        def t_ab(t):
            i = t.matmul(ps_qk[0][:, 0:8], ones_row[0:1, 0:128],
                         alpha_row[0:1, :], start=True, stop=True)
            pe.inc(i)
        P.on("tensor", t_ab)
        ab_pe = pe.n
        P.wait("vector", pe, ab_pe)
        P.wait("vector", d_misc, misc_n)

        def v_ab(vv):
            i = vv.tensor_copy(alphab_sb[:, :], ps_qk[0][:, 0:8])
        P.on("vector", v_ab)
        P.set_guard("ps_qk0", ve, ve.n)

        gcol = lambda k: eg_sb[:, k * 8:(k + 1) * 8]
        for tb in range(n_gtb):
            sl = tb % 2
            ps = ps_qk[sl]
            P.guard_wait("tensor", f"ps_qk{sl}")

            def t_g(t, tb=tb, ps=ps):
                for it in range(NI):
                    i = t.matmul(ps[:, 0:8],
                                 xnT_sb[:, it * S + tb * 128:
                                        it * S + (tb + 1) * 128],
                                 gate_sb[:, it * 8:(it + 1) * 8],
                                 start=(it == 0), stop=(it == NI - 1))
                pe.inc(i)
            P.on("tensor", t_g)
            g_pe = pe.n

            P.wait("vector", pe, g_pe)
            P.guard_wait("vector", "gate_cols")

            def v_g1(vv, ps=ps):
                vv.tensor_copy(gcol(0), ps[:, 0:8])
                vv.reduce_max(st12[:, 0:1], gcol(0), axis=AX.X)
                i = vv.reduce_max(st12[:, 1:2], gcol(0), axis=AX.X, negate=True)
            P.on("vector", v_g1)
            g1_ve = ve.n
            P.set_guard(f"ps_qk{sl}", ve, g1_ve)

            P.wait("scalar", ve, g1_ve)

            def sc_g(sc):
                i = sc.activation(gcol(1), gcol(0), AF.Exp, bias=st12[:, 1:2])
            P.on("scalar", sc_g)
            g_se = se.n

            P.wait("vector", se, g_se)

            def v_g2(vv):
                vv.tensor_scalar(gcol(2), gcol(0), st12[:, 0:1], None,
                                 ALU.is_equal)
                vv.scalar_tensor_tensor(gcol(2), gcol(2), -1e30, gcol(0),
                                        ALU.mult, ALU.add)
                i = vv.reduce_max(st12[:, 2:3], gcol(2), axis=AX.X)
            P.on("vector", v_g2)
            P.wait("vector", ve, ve.n)

            def v_g3(vv):
                vv.tensor_scalar(gcol(3), gcol(0), st12[:, 2:3], None, ALU.is_ge)
                vv.tensor_mul(gcol(1), gcol(1), gcol(3))
                i = vv.reduce_sum(st12[:, 3:4], gcol(1), axis=AX.X)
            P.on("vector", v_g3)
            P.wait("vector", ve, ve.n)

            def v_g4(vv):
                i = vv.reciprocal(st12[:, 4:5], st12[:, 3:4])
            P.on("vector", v_g4)
            P.wait("vector", ve, ve.n)

            def v_g5(vv, tb=tb):
                vv.tensor_scalar(gcol(1), gcol(1), st12[:, 4:5], None, ALU.mult)
                i = vv.tensor_mul(comb_sb[:, tb * 8:(tb + 1) * 8],
                                  gcol(1), alphab_sb[:, :])
            P.on("vector", v_g5)
            P.set_guard("gate_cols", ve, ve.n)

        P.wait("scalar", pe, pe.n)

        def sc_spxn(sc):
            i = sc.dma_start(out=xnT_d.rearrange("n p s -> p n s"),
                             in_=xnT_sb[:, :].rearrange("p (n s) -> p n s",
                                                        n=NI))
            d_sp.inc(i, 16)
        P.on("scalar", sc_spxn)
        sp_all = d_sp.n

        p2_pe, p2_ve, p2_se = pe.n, ve.n, se.n
        a12.close()

        a3 = ExitStack()
        kt_sb = a3.enter_context(nc.sbuf_tensor("kt_sb", [128, 2 * S], BF16))
        vh_sb = a3.enter_context(nc.sbuf_tensor("vh_sb", [128, 2 * S], BF16))
        qh_sb = a3.enter_context(nc.sbuf_tensor("qh_sb", [128, 2 * S], BF16))
        p_sb = a3.enter_context(nc.sbuf_tensor("p_sb", [128, 8 * S], BF16))
        pT_sb = a3.enter_context(nc.sbuf_tensor("pT_sb", [128, 16 * 512], BF16))
        cte_sb = a3.enter_context(nc.sbuf_tensor("cte_sb", [128, 2 * 512], BF16))
        st3 = a3.enter_context(nc.sbuf_tensor("st3", [128, 16], F32))
        q2_sb = a3.enter_context(nc.sbuf_tensor("q2_sb", [128, S], BF16))
        bnd_sb = a3.enter_context(nc.sbuf_tensor("bnd_sb", [128, 2 * NT], F32))
        ps_S = psA
        ps_tr3 = psB[:, 0:1024].bitcast(BF16)
        ps_ctx = psB[:, 1024:2048]

        for eng in ("tensor", "vector", "scalar", "sync"):
            P.wait(eng, pe, p2_pe)
            P.wait(eng, ve, p2_ve)
            P.wait(eng, se, p2_se)
        P.wait("sync", d_sp, sp_all)
        P.wait("vector", d_mk, mask_rdy)

        pT_view = pT_sb[:, :].rearrange("p (k c) -> p k c", k=16)

        groups = [(h, g) for h in range(n_h) for g in range(4)]
        pendingB = []

        def flushB(n):
            for _ in range(min(n, len(pendingB))):
                pendingB.pop(0)()

        def plan_kv_load(h):
            ksl = h % 2
            dk = d_kA if ksl == 0 else d_kB
            kth = kt_sb[:, ksl * S:(ksl + 1) * S]
            vh = vh_sb[:, ksl * S:(ksl + 1) * S]
            P.guard_wait("sync", f"kv{ksl}")

            def s_kv(sy, h=h, kth=kth, vh=vh, dk=dk):
                i = sy.dma_start(out=kth, in_=kT_d[h, :, :])
                dk.inc(i, 16)
                i = sy.dma_start(out=vh.rearrange("p (k c) -> p k c", k=NT),
                                 in_=v_d[:, :, h * 128:(h + 1) * 128]
                                 .rearrange("k p c -> p k c"))
                dk.inc(i, 16)
            P.on("sync", s_kv)
            return dk.n

        def plan_q_load(hq):
            qsl = hq % 2
            dq = d_qA if qsl == 0 else d_qB
            qh = qh_sb[:, qsl * S:(qsl + 1) * S]
            P.guard_wait("sync", f"q{qsl}")

            def s_q(sy, hq=hq, qh=qh, dq=dq):
                i = sy.dma_start(out=qh, in_=qT_d[hq, :, :])
                dq.inc(i, 16)
            P.on("sync", s_q)
            return dq.n

        def plan_bnd(hh):
            hpar = hh % 2
            dqh = d_qA if hpar == 0 else d_qB
            qhf = qh_sb[:, hpar * S:(hpar + 1) * S]
            P.wait("vector", dqh, q_rdys[hh])
            P.guard_wait("vector", "q2buf")

            def v_q2(vv, qhf=qhf):
                i = vv.tensor_mul(q2_sb[:, :], qhf, qhf)
            P.on("vector", v_q2)
            q2_ve = ve.n
            P.wait("tensor", ve, q2_ve)
            P.guard_wait("tensor", "ps_S0")

            def t_qn(t):
                for tb in range(NT):
                    i = t.matmul(ps_S[:, tb:tb + 1],
                                 q2_sb[:, tb * 128:(tb + 1) * 128],
                                 ones_col[:, :], start=True, stop=True)
                pe.inc(i)
            P.on("tensor", t_qn)
            qn_pe = pe.n
            P.set_guard("q2buf", pe, qn_pe)
            P.wait("vector", pe, qn_pe)
            P.guard_wait("vector", f"bndbuf{hpar}")

            def v_bnd(vv, hh=hh, hpar=hpar):
                i = vv.tensor_scalar(bnd_sb[:, hpar * NT:(hpar + 1) * NT],
                                     ps_S[:, 0:NT],
                                     kn2b_sb[:, hh:hh + 1], -0.5 * SCALE,
                                     ALU.add, ALU.mult)
            P.on("vector", v_bnd)
            bnd_ve = ve.n
            P.set_guard("ps_S0", ve, bnd_ve)
            P.wait("scalar", ve, bnd_ve)

        kv_rdys = {0: plan_kv_load(0)}
        q_rdys = {0: plan_q_load(0)}
        bank_cur = 0

        for gi, (h, g) in enumerate(groups):
            par = gi % 2
            ksl = h % 2
            dk = d_kA if ksl == 0 else d_kB
            vh = vh_sb[:, ksl * S:(ksl + 1) * S]
            kth = kt_sb[:, ksl * S:(ksl + 1) * S]

            if g == 1 and h + 1 < n_h:
                kv_rdys[h + 1] = plan_kv_load(h + 1)
                q_rdys[h + 1] = plan_q_load(h + 1)
            kv_rdy = kv_rdys[h]
            q_rdy = q_rdys[h]
            qsl = h % 2
            dq = d_qA if qsl == 0 else d_qB
            qg = qh_sb[:, qsl * S + g * 512:qsl * S + (g + 1) * 512]

            if gi == 0:
                plan_bnd(0)
            if g == 2 and h + 1 < n_h:
                plan_bnd(h + 1)

            P.guard_wait("scalar", f"pbuf{par}")

            for qbi in range(4):
                qb = 4 * g + qbi
                K = (qb + 1) * 128
                nbs = (K + 511) // 512
                pq = p_sb[:, (par * 4 + qbi) * S:(par * 4 + qbi) * S + S]

                P.wait("tensor", dq, q_rdy)
                P.wait("tensor", dk, kv_rdy)

                ab = 8 + 4 * (qbi % 2)
                P.guard_wait("scalar", f"lcols{qbi % 2}")
                for nb in range(nbs):
                    n0 = nb * 512
                    n1 = min(K, n0 + 512)
                    bk = (bank_cur + nb) % 4
                    b0 = bk * 512
                    diag = (n1 == K)
                    P.guard_wait("tensor", f"ps_S{bk}")

                    def t_s(t, qbi=qbi, qb=qb, n0=n0, n1=n1, b0=b0, qg=qg,
                            kth=kth, diag=diag):
                        i = t.matmul(ps_S[:, b0:b0 + (n1 - n0)],
                                     qg[:, qbi * 128:(qbi + 1) * 128],
                                     kth[:, n0:n1], start=True, stop=not diag)
                        if diag:
                            d0 = b0 + qb * 128 - n0
                            i = t.matmul(ps_S[:, d0:d0 + 128], ident_sb[:, :],
                                         maskb_sb[:, :], start=False,
                                         stop=True)
                        pe.inc(i)
                    P.on("tensor", t_s)
                    s_pe = pe.n
                    P.wait("scalar", pe, s_pe)

                    def sc_exp(sc, pq=pq, n0=n0, n1=n1, b0=b0, qb=qb, nb=nb,
                               ab=ab):
                        i = sc.activation(pq[:, n0:n1],
                                          ps_S[:, b0:b0 + (n1 - n0)],
                                          AF.Exp,
                                          bias=bnd_sb[:, (h % 2) * NT + qb:
                                                      (h % 2) * NT + qb + 1],
                                          scale=SCALE,
                                          accum_out=st3[:, ab + nb:ab + nb + 1])
                    P.on("scalar", sc_exp)
                    P.set_guard(f"ps_S{bk}", se, se.n)
                bank_cur = (bank_cur + nbs) % 4
                if qbi == 3:
                    P.set_guard(f"q{qsl}", pe, pe.n)
                if qbi > 0:
                    flushB(1)

                exp_se = se.n
                if qbi == 3 and g == 3:
                    P.set_guard(f"bndbuf{h % 2}", se, exp_se)
                P.wait("vector", se, exp_se)

                def v_li(vv, nbs=nbs, ab=ab, qbi=qbi):
                    a = st3[:, ab:ab + 1]
                    for nb in range(1, nbs):
                        vv.tensor_add(a, a, st3[:, ab + nb:ab + nb + 1])
                    i = vv.reciprocal(st3[:, 2 + qbi % 2:3 + qbi % 2], a)
                P.on("vector", v_li)
                P.set_guard(f"lcols{qbi % 2}", ve, ve.n)
                P.wait("vector", ve, ve.n)

                def v_ps2(vv, pq=pq, K=K, qbi=qbi):
                    i = vv.tensor_scalar(pq[:, 0:K], pq[:, 0:K],
                                         st3[:, 2 + qbi % 2:3 + qbi % 2],
                                         None, ALU.mult)
                P.on("vector", v_ps2)

            pscale_ve = ve.n

            trv = [0]

            def mk_tr_piece(kts, h=h, g=g, par=par, pscale_ve=pscale_ve,
                            last=False, trv=trv):
                def piece():
                    P.wait("tensor", ve, pscale_ve)
                    for kt in kts:
                        q0 = max(0, kt - 4 * g)
                        cnt = 4 - q0
                        ph = kt % 2
                        P.guard_wait("tensor", f"ps_tr3{ph}")

                        def t_ptr(t, kt=kt, q0=q0, cnt=cnt, ph=ph, par=par):
                            for idx in range(cnt):
                                qbi = q0 + idx
                                sl = (par * 4 + qbi) * S
                                i = t.transpose(
                                    ps_tr3[:, ph * 1024 + idx * 128:
                                           ph * 1024 + (idx + 1) * 128],
                                    p_sb[:, sl + kt * 128:sl + (kt + 1) * 128],
                                    ident_sb[:, :])
                            pe.inc(i)
                        P.on("tensor", t_ptr)
                        ptr_pe = pe.n

                        P.wait("vector", pe, ptr_pe)

                        def v_pev(vv, kt=kt, q0=q0, cnt=cnt, ph=ph):
                            i = vv.tensor_copy(
                                pT_view[:, kt, q0 * 128:(q0 + cnt) * 128],
                                ps_tr3[:, ph * 1024:ph * 1024 + cnt * 128])
                        P.on("vector", v_pev)
                        P.set_guard(f"ps_tr3{ph}", ve, ve.n)
                    if last:
                        P.set_guard(f"pbuf{par}", pe, pe.n)
                        trv[0] = ve.n
                return piece

            def mk_ctx_piece(h=h, g=g, gi=gi, vh=vh, ksl=ksl, trv=trv):
                def piece():
                    nkt = 4 * g + 4
                    csl = gi % 2
                    P.wait("tensor", ve, trv[0])
                    P.guard_wait("tensor", f"ps_ctx{csl}")

                    def t_ctx(t, g=g, nkt=nkt, vh=vh, csl=csl):
                        for kt in range(nkt):
                            q0 = max(0, kt - 4 * g)
                            i = t.matmul(
                                ps_ctx[:, csl * 512 + q0 * 128:csl * 512 + 512],
                                vh[:, kt * 128:(kt + 1) * 128],
                                pT_view[:, kt, q0 * 128:512],
                                start=(kt == 0), stop=(kt == nkt - 1))
                        pe.inc(i)
                    P.on("tensor", t_ctx)
                    if g == 3:
                        P.set_guard(f"kv{ksl}", pe, pe.n)
                return piece

            def mk_ev_piece(h=h, g=g, gi=gi):
                def piece():
                    csl = gi % 2
                    ctx_pe = pe.n
                    P.wait("vector", pe, ctx_pe)
                    P.wait("vector", d_sp2, d_sp2.n)

                    def v_cev(vv, csl=csl):
                        i = vv.tensor_copy(cte_sb[:, csl * 512:(csl + 1) * 512],
                                           ps_ctx[:, csl * 512:(csl + 1) * 512])
                    P.on("vector", v_cev)
                    cev_ve = ve.n
                    P.set_guard(f"ps_ctx{csl}", ve, cev_ve)

                    P.wait("gpsimd", ve, cev_ve)

                    def s_cts(gp, h=h, g=g, csl=csl):
                        i = gp.dma_start(out=ctxT_d[h, :,
                                                    g * 512:(g + 1) * 512],
                                         in_=cte_sb[:, csl * 512:
                                                    (csl + 1) * 512])
                        d_sp2.inc(i, 16)
                    P.on("gpsimd", s_cts)
                return piece

            nkt = 4 * g + 4
            half = (nkt + 1) // 2
            flushB(len(pendingB))
            pendingB = [
                mk_tr_piece(list(range(half))),
                mk_tr_piece(list(range(half, nkt)), last=True),
                mk_ctx_piece(),
                mk_ev_piece(),
            ]
            if gi == len(groups) - 1:
                flushB(4)
        p3_sp = d_sp.n
        p3_sp2 = d_sp2.n

        p3_pe, p3_ve, p3_se = pe.n, ve.n, se.n
        a3.close()

        a4 = ExitStack()
        wo_sb = a4.enter_context(nc.sbuf_tensor("wo_sb", [128, NI * E], BF16))
        ct_sb = a4.enter_context(nc.sbuf_tensor("ct_sb", [128, 2 * E], BF16))
        x4_sb = a4.enter_context(nc.sbuf_tensor("x4_sb", [128, 2 * E], F32))
        o4_sb = a4.enter_context(nc.sbuf_tensor("o4_sb", [128, 2 * E], F32))
        ps_o = [psA, psB]

        for eng in ("tensor", "vector", "scalar", "sync", "gpsimd"):
            P.wait(eng, pe, p3_pe)
            P.wait(eng, ve, p3_ve)
            P.wait(eng, se, p3_se)
        P.wait("gpsimd", d_sp, p3_sp)
        P.wait("gpsimd", d_sp2, p3_sp2)

        def g_wo(gp):
            for it in range(NI):
                i = gp.dma_start(out=wo_sb[:, it * E:(it + 1) * E],
                                 in_=wout_v[:, it, :])
                d_wv.inc(i, 16)
        P.on("gpsimd", g_wo)
        wo_rdy = d_wv.n

        P.wait("sync", d_sp, p3_sp)
        P.wait("sync", d_sp2, p3_sp2)
        for tb in range(n_p4):
            sl = tb % 2
            dc = d_cA if sl == 0 else d_cB
            dx = d_xA if sl == 0 else d_xB
            ctv = ct_sb[:, sl * E:(sl + 1) * E].rearrange(
                "p (k c) -> p k c", k=NI)
            x4 = x4_sb[:, sl * E:(sl + 1) * E]
            o4 = o4_sb[:, sl * E:(sl + 1) * E]
            ps = ps_o[sl]

            P.guard_wait("sync", f"ct{sl}")

            def s_ct(sy, tb=tb, ctv=ctv, dc=dc):
                i = sy.dma_start(out=ctv,
                                 in_=ctxT_d[:, :, tb * 128:(tb + 1) * 128]
                                 .rearrange("k p c -> p k c"))
                dc.inc(i, 16)
            P.on("sync", s_ct)
            ct_rdy = dc.n

            P.guard_wait("sync", f"x4{sl}")

            def s_x4(sy, tb=tb, x4=x4, dx=dx):
                i = sy.dma_start(out=x4, in_=x_v[tb, :, :])
                dx.inc(i, 16)
            P.on("sync", s_x4)
            x4_rdy = dx.n

            P.wait("tensor", dc, ct_rdy)
            if tb < 2:
                P.wait("tensor", d_wv, wo_rdy)
            P.guard_wait("tensor", f"ps_o{sl}")

            def t_o(t, ctv=ctv, ps=ps):
                for it in range(NI):
                    lhsT = ctv[:, it, :]
                    for nb in range(4):
                        t.matmul(ps[:, nb * 512:(nb + 1) * 512], lhsT,
                                 wo_sb[:, it * E + nb * 512:
                                       it * E + (nb + 1) * 512],
                                 start=(it == 0), stop=False)
                for nb in range(4):
                    i = t.matmul(ps[:, nb * 512:(nb + 1) * 512],
                                 ones_row[0:1, 0:128],
                                 bout_row[0:1, nb * 512:(nb + 1) * 512],
                                 start=False, stop=True)
                pe.inc(i)
            P.on("tensor", t_o)
            o_pe = pe.n
            P.set_guard(f"ct{sl}", pe, o_pe)

            P.wait("vector", pe, o_pe)
            P.wait("vector", dx, x4_rdy)
            P.wait("vector", d_out, d_out.n)

            def v_o(vv, ps=ps, x4=x4, o4=o4):
                i = vv.tensor_add(o4, ps[:, :], x4)
            P.on("vector", v_o)
            o_ve = ve.n
            P.set_guard(f"ps_o{sl}", ve, o_ve)
            P.set_guard(f"x4{sl}", ve, o_ve)

            P.wait("sync", ve, o_ve)

            def s_o(sy, tb=tb, o4=o4):
                i = sy.dma_start(out=out_v[tb, :, :], in_=o4)
                d_out.inc(i, 16)
            P.on("sync", s_o)
        p4_out = d_out.n

        p4_pe, p4_ve, p4_se = pe.n, ve.n, se.n
        a4.close()

        a5 = ExitStack()
        xnc_sb = a5.enter_context(nc.sbuf_tensor("xnc_sb", [128, NI * CHUNK], BF16))
        acc_sb = a5.enter_context(nc.sbuf_tensor("acc_sb", [128, 8 * E], F32))
        hT_sb = a5.enter_context(nc.sbuf_tensor("hT_sb", [128, NI * CHUNK], BF16))
        f2_sb = a5.enter_context(nc.sbuf_tensor("f2_sb", [128, 8 * E], BF16))
        f1_sb = a5.enter_context(nc.sbuf_tensor("f1_sb", [128, 4 * E], BF16))
        b1_row = a5.enter_context(nc.sbuf_tensor("b1_row", [1, E], BF16))
        b2_row = a5.enter_context(nc.sbuf_tensor("b2_row", [1, E], BF16))
        ps_h = psA
        ps_f = psB

        for eng in ("tensor", "vector", "scalar", "sync", "gpsimd"):
            P.wait(eng, pe, p4_pe)
            P.wait(eng, ve, p4_ve)
            P.wait(eng, se, p4_se)

        out_ch = out.rearrange("(tb p) j -> tb p j", p=128)

        for c in range(n_ch):
            P.guard_wait("sync", "xnc")
            P.wait("sync", d_sp, d_sp.n)

            def s_xnc(sy, c=c):
                i = sy.dma_start(
                    out=xnc_sb[:, :].rearrange("p (n s) -> p n s", n=NI),
                    in_=xnT_d[:, :, c * CHUNK:(c + 1) * CHUNK]
                    .rearrange("n p s -> p n s"))
                d_mc.inc(i, 16)
            P.on("sync", s_xnc)
            xnc_rdy = d_mc.n

            P.wait("sync", d_out, d_out.n)

            def s_acc(sy, c=c):
                i = sy.dma_start(
                    out=acc_sb[:, :].rearrange("p (tb j) -> p tb j", tb=8),
                    in_=out_ch[c * 8:(c + 1) * 8, :, :]
                    .rearrange("t p j -> p t j"))
                d_mc.inc(i, 16)
            P.on("sync", s_acc)
            acc_rdy = d_mc.n
            xnc_rdy = d_mc.n
            P.wait("vector", d_mc, acc_rdy)

            for e in range(NEXP):
                P.guard_wait("gpsimd", "b12")

                def g_b(gp, e=e):
                    i = gp.dma_start(out=b1_row[:, :], in_=fc1_b[e, :][None, :])
                    d_gb.inc(i, 16)
                    i = gp.dma_start(out=b2_row[:, :], in_=fc2_b[e, :][None, :])
                    d_gb.inc(i, 16)
                P.on("gpsimd", g_b)
                b_rdy = d_gb.n

                f1_rdys = {}

                def plan_f1_fill(ib, e=e):
                    sl = ib % 4
                    df = (d_f1A, d_f1B, d_f1C, d_f1D)[sl]
                    f1 = f1_sb[:, sl * E:(sl + 1) * E]
                    P.guard_wait("gpsimd", f"f1{sl}")

                    def g_f1(gp, e=e, ib=ib, f1=f1, df=df):
                        i = gp.dma_start(
                            out=f1.rearrange("p (dt j) -> p dt j", dt=NI),
                            in_=fc1_v[e, :, :, ib * 128:(ib + 1) * 128])
                        df.inc(i, 16)
                    P.on("gpsimd", g_f1)
                    f1_rdys[ib] = df.n

                def plan_f2_tile(half, it8, dfh, e=e):
                    if it8 == 0:
                        P.guard_wait("gpsimd", "f2buf")

                    def g_f2(gp, e=e, half=half, it8=it8, dfh=dfh):
                        i = gp.dma_start(
                            out=f2_sb[:, it8 * E:(it8 + 1) * E],
                            in_=fc2_v[e, :, half * 8 + it8, :])
                        dfh.inc(i, 16)
                    P.on("gpsimd", g_f2)
                    return dfh.n

                def plan_fc1(ib, e=e):
                    sl = ib % 4
                    df = (d_f1A, d_f1B, d_f1C, d_f1D)[sl]
                    f1 = f1_sb[:, sl * E:(sl + 1) * E]

                    P.wait("tensor", df, f1_rdys[ib])
                    if e == 0 and ib < 2:
                        P.wait("tensor", d_mc, xnc_rdy)
                    if ib == 0:
                        P.wait("tensor", d_gb, b_rdy)
                    psl = ib % 2
                    P.guard_wait("tensor", f"ps_h{psl}")

                    def t_f1(t, ib=ib, f1=f1, psl=psl):
                        psh = ps_h[:, psl * 1024:(psl + 1) * 1024]
                        for dt in range(NI):
                            lhsT = f1[:, dt * 128:(dt + 1) * 128]
                            for nb in range(2):
                                t.matmul(psh[:, nb * 512:(nb + 1) * 512], lhsT,
                                         xnc_sb[:, dt * CHUNK + nb * 512:
                                                dt * CHUNK + (nb + 1) * 512],
                                         start=(dt == 0), stop=False)
                        for nb in range(2):
                            i = t.matmul(psh[:, nb * 512:(nb + 1) * 512],
                                         b1_row[0:1, ib * 128:(ib + 1) * 128],
                                         ones_row[0:1, nb * 512:(nb + 1) * 512],
                                         start=False, stop=True)
                        pe.inc(i)
                    P.on("tensor", t_f1)
                    f1_pe = pe.n
                    P.set_guard(f"f1{sl}", pe, f1_pe)

                    P.wait("scalar", pe, f1_pe)
                    P.guard_wait("scalar", "hT_lo" if ib < 8 else "hT_hi")

                    def sc_gel(sc, ib=ib, psl=psl):
                        i = sc.activation(hT_sb[:, ib * CHUNK:(ib + 1) * CHUNK],
                                          ps_h[:, psl * 1024:(psl + 1) * 1024],
                                          AF.Gelu)
                    P.on("scalar", sc_gel)
                    P.set_guard(f"ps_h{psl}", se, se.n)

                def plan_fc2(half, f2_rdy, gel_se, e=e, c=c):
                    dfh = d_f2A if half == 0 else d_f2B
                    P.wait("tensor", dfh, f2_rdy)
                    P.wait("tensor", se, gel_se)
                    for tb in range(8):
                        for jh in range(2):
                            P.guard_wait("tensor", f"ps_f{jh}")

                            def t_f2(t, tb=tb, half=half, jh=jh):
                                psf = ps_f[:, jh * 1024:(jh + 1) * 1024]
                                last = None
                                for it8 in range(8):
                                    it = half * 8 + it8
                                    lhsT = hT_sb[:, it * CHUNK + tb * 128:
                                                 it * CHUNK + (tb + 1) * 128]
                                    for nb in range(2):
                                        j0 = jh * 1024 + nb * 512
                                        stop = (half == 1 and it8 == 7)
                                        last = t.matmul(
                                            psf[:, nb * 512:(nb + 1) * 512],
                                            lhsT,
                                            f2_sb[:, it8 * E + j0:
                                                  it8 * E + j0 + 512],
                                            start=(it8 == 0), stop=stop)
                                if half == 0:
                                    for nb in range(2):
                                        j0 = jh * 1024 + nb * 512
                                        last = t.matmul(
                                            psf[:, nb * 512:(nb + 1) * 512],
                                            ones_row[0:1, 0:128],
                                            b2_row[0:1, j0:j0 + 512],
                                            start=False, stop=True)
                                pe.inc(last)
                            P.on("tensor", t_f2)
                            f2_pe = pe.n

                            P.wait("vector", pe, f2_pe)

                            def v_acc(vv, c=c, e=e, tb=tb, jh=jh):
                                cidx = (c * 8 + tb) * 8 + e
                                csl = comb_sb[:, cidx:cidx + 1]
                                at = acc_sb[:, tb * E + jh * 1024:
                                            tb * E + (jh + 1) * 1024]
                                i = vv.scalar_tensor_tensor(
                                    at, ps_f[:, jh * 1024:(jh + 1) * 1024],
                                    csl, at, ALU.mult, ALU.add)
                            P.on("vector", v_acc)
                            P.set_guard(f"ps_f{jh}", ve, ve.n)
                    P.set_guard("f2buf", pe, pe.n)
                    P.set_guard("hT_lo" if half == 0 else "hT_hi", pe, pe.n)

                for ib in range(4):
                    plan_f1_fill(ib)
                f2_rdy0 = plan_f2_tile(0, 0, d_f2A)
                for k in range(8):
                    plan_fc1(k)
                    if k < 4:
                        plan_f1_fill(k + 4)
                    if k < 7:
                        f2_rdy0 = plan_f2_tile(0, k + 1, d_f2A)
                gel_lo = se.n
                plan_fc2(0, f2_rdy0, gel_lo)

                for ib in range(8, 12):
                    plan_f1_fill(ib)
                f2_rdy1 = plan_f2_tile(1, 0, d_f2B)
                for k in range(8, NI):
                    plan_fc1(k)
                    if k < 12:
                        plan_f1_fill(k + 4)
                    if k < 15:
                        f2_rdy1 = plan_f2_tile(1, k - 7, d_f2B)
                gel_hi = se.n
                P.set_guard("b12", pe, pe.n)
                plan_fc2(1, f2_rdy1, gel_hi)

            acc_ve = ve.n
            P.wait("sync", ve, acc_ve)

            def s_fin(sy, c=c):
                i = sy.dma_start(out=out_ch[c * 8:(c + 1) * 8, :, :]
                                 .rearrange("t p j -> p t j"),
                                 in_=acc_sb[:, :].rearrange(
                                     "p (tb j) -> p tb j", tb=8))
                d_out.inc(i, 16)
            P.on("sync", s_fin)
            P.set_guard("xnc", pe, pe.n)

        a5.close()

        P.wait("sync", d_out, d_out.n)
        P.wait("sync", d_sp, d_sp.n)

        with nc.Block() as block:
            @block.tensor
            def _(t):
                for fn in P.q["tensor"]:
                    fn(t)

            @block.vector
            def _(v):
                for fn in P.q["vector"]:
                    fn(v)

            @block.scalar
            def _(s):
                for fn in P.q["scalar"]:
                    fn(s)

            @block.sync
            def _(s):
                for fn in P.q["sync"]:
                    fn(s)

            @block.gpsimd
            def _(g):
                for fn in P.q["gpsimd"]:
                    fn(g)

    return nc


_MASK = None
_IDENT = None


def _consts():
    global _MASK, _IDENT
    if _MASK is None:
        m = np.zeros((128, 128), dtype=np.float32)
        iu = np.triu_indices(128, k=1)
        m[iu] = NEG
        _MASK = m
        _IDENT = np.eye(128, dtype=np.float32)
    return _MASK, _IDENT


_NC_CACHE = None


def kernel(L0_attn_inX, L0_attn_W_qkv, L0_attn_b_qkv, L0_attn_W_out,
           L0_attn_b_out, L0_ffn_gate_w, L0_ffn_fc1_w, L0_ffn_fc1_b,
           L0_ffn_fc2_w, L0_ffn_fc2_b, L0_ffn_alpha):
    global _NC_CACHE
    x = np.ascontiguousarray(np.asarray(L0_attn_inX, dtype=np.float32))
    mask, ident = _consts()
    shared = {
        "w_qkv": np.ascontiguousarray(np.asarray(L0_attn_W_qkv, np.float32)),
        "b_qkv": np.ascontiguousarray(np.asarray(L0_attn_b_qkv, np.float32)),
        "w_out": np.ascontiguousarray(np.asarray(L0_attn_W_out, np.float32)),
        "b_out": np.ascontiguousarray(np.asarray(L0_attn_b_out, np.float32)),
        "gate_w": np.ascontiguousarray(np.asarray(L0_ffn_gate_w, np.float32)),
        "fc1_w": np.ascontiguousarray(np.asarray(L0_ffn_fc1_w, np.float32)),
        "fc1_b": np.ascontiguousarray(np.asarray(L0_ffn_fc1_b, np.float32)),
        "fc2_w": np.ascontiguousarray(np.asarray(L0_ffn_fc2_w, np.float32)),
        "fc2_b": np.ascontiguousarray(np.asarray(L0_ffn_fc2_b, np.float32)),
        "alpha": np.ascontiguousarray(np.asarray(L0_ffn_alpha, np.float32)),
        "cmask": mask, "ident": ident,
    }
    if _NC_CACHE is None:
        _NC_CACHE = build()
    nc = _NC_CACHE
    in_maps = [dict(shared, x=np.ascontiguousarray(x[b])) for b in range(8)]
    res = run_bass_kernel_spmd(nc, in_maps, core_ids=list(range(8)))
    return np.stack([r["out"] for r in res.results], axis=0)
